# version 49
# baseline (speedup 1.0000x reference)
import sys
from contextlib import ExitStack
import numpy as np
import ml_dtypes
import concourse.bass as bass
import concourse.mybir as mybir
from concourse.bass_utils import run_bass_kernel_spmd

F32 = mybir.dt.float32
BF16 = mybir.dt.bfloat16
AF = mybir.ActivationFunctionType
ALU = mybir.AluOpType
AX = mybir.AxisListType.X
MAGIC = float(np.float32(3 * 2.0 ** 22))
C_STAB = 16.0


class CFG:
    def __init__(self, S=2048, D=1024, H=16, HD=64, INNER=4096, SEQS=2,
                 debug=False):
        self.debug = debug
        assert HD == 64
        self.S, self.D, self.H, self.HD, self.INNER, self.SEQS = S, D, H, HD, INNER, SEQS
        self.TC = S // 128
        self.QT = S // 512
        self.KC = S // 128
        self.PAIRS = H // 2
        self.DC = D // 128
        self.JT = D // 512
        self.OC = INNER // 128
        self.OTILE = INNER // 512
        self.TSB = min(4, self.TC)
        self.NTSB = self.TC // self.TSB
        assert self.PAIRS == self.DC


class Plan:
    ENGINES = ("sync", "tensor", "scalar", "vector")

    def __init__(self, nc):
        self.nc = nc
        self.ops = {e: [] for e in self.ENGINES}
        self.sems = {}
        self.counts = {}

    def sem(self, name):
        if name not in self.sems:
            self.sems[name] = self.nc.alloc_semaphore(name="sem_" + name)
            self.counts[name] = 0
        return name

    def add(self, eng, emit, waits=(), inc=None):
        red = {}
        for w in waits:
            if w is None:
                continue
            s, v = w
            if red.get(s, -1) < v:
                red[s] = v
        tok = None
        if inc is not None:
            s, k = inc
            self.sem(s)
            self.counts[s] += k
            tok = (s, self.counts[s])
        for s in red:
            self.sem(s)
        fr = sys._getframe(2)
        self.ops[eng].append((emit, tuple(red.items()), inc,
                              f"{fr.f_code.co_name}:{fr.f_lineno}"))
        return tok

    def emit(self):
        plans = self
        with self.nc.Block() as blk:
            def make_body(eng_name):
                def body(eng):
                    last = {}
                    for emit, waits, inc, tag in plans.ops[eng_name]:
                        for s, v in waits:
                            if last.get(s, -1) >= v:
                                continue
                            eng.wait_ge(plans.sems[s], v)
                            last[s] = v
                        try:
                            instr = emit(eng) if emit is not None else None
                        except Exception:
                            print(f"EMIT FAILED at {tag} on {eng_name}")
                            raise
                        if inc is not None:
                            s, k = inc
                            instr.then_inc(plans.sems[s], k)
                return body
            for e in self.ENGINES:
                if self.ops[e]:
                    getattr(blk, e)(make_body(e))


class Builder:
    NDMA = 16

    def __init__(self, nc, cfg: CFG):
        self.nc, self.c = nc, cfg
        self.p = Plan(nc)
        self.dma_sems = [f"dma{i}" for i in range(self.NDMA)]
        for s in self.dma_sems + ["pe", "act", "dve"]:
            self.p.sem(s)
        self._dma_rot = 0
        self._names = {}
        self.dma_floor = None
        self.t_wsplit_done = []
        self.t_qrep_done = None
        self.ab_user = {}
        self.og_store = [None, None]
        self.nrm_prev = None
        self.last_score_tok = None
        self.last_av_tok = None
        self.avslot = 0
        self.norm_pending = None
        self.t_ot_store = None

    def nm(self, base):
        i = self._names.get(base, 0)
        self._names[base] = i + 1
        return f"{base}_{i}" if i else base

    def sb(self, name, shape, dtype, **kw):
        return self.nc.alloc_sbuf_tensor(self.nm(name), shape, dtype, **kw).ap()

    def sbg(self, name, shape, dtype, **kw):
        return self.nc.sbuf_tensor(self.nm(name), shape, dtype, **kw)

    def dram(self, name, shape, dtype, kind="Internal"):
        return self.nc.dram_tensor(self.nm(name), shape, dtype, kind=kind).ap()

    def dma(self, out_ap, in_ap, waits=()):
        sem = self.dma_sems[self._dma_rot]
        self._dma_rot = (self._dma_rot + 1) % self.NDMA
        w = list(waits) + [(sem, self.p.counts[sem])]
        if self.dma_floor is not None:
            w.append(self.dma_floor)
        return self.p.add("sync", lambda e, o=out_ap, i=in_ap:
                          e.dma_start(out=o, in_=i), w, (sem, 16))

    def dmat(self, out_ap, in_ap, waits=()):
        sem = self.dma_sems[self._dma_rot]
        self._dma_rot = (self._dma_rot + 1) % self.NDMA
        fence = [(ss, self.p.counts[ss]) for ss in self.dma_sems]
        tok = self.p.add("sync", lambda e, o=out_ap, i=in_ap:
                         e.dma_start_transpose(o, i),
                         list(waits) + fence, (sem, 16))
        self.dma_floor = tok
        return tok

    def mm(self, out_ap, lhsT, rhs, start, stop, waits=(), inc=True,
           force_inc=False):
        do_inc = force_inc or (inc and stop)
        return self.p.add("tensor",
                          lambda e, o=out_ap, l=lhsT, r=rhs, a=start, b=stop:
                          e.matmul(o, l, r, start=a, stop=b),
                          waits, ("pe", 1) if do_inc else None)

    def tr(self, out_psum, in_sbuf, waits=()):
        return self.p.add("tensor",
                          lambda e, o=out_psum, i=in_sbuf, d=self.ident:
                          e.transpose(o, i, d), waits, ("pe", 1))

    def act(self, out_ap, in_ap, func, bias=0.0, scale=1.0, accum_out=None, waits=()):
        return self.p.add("scalar",
                          lambda e, o=out_ap, i=in_ap, f=func, b=bias, s=scale, a=accum_out:
                          e.activation(o, i, f, bias=b, scale=s, accum_out=a),
                          waits, ("act", 1))

    def ts(self, out_ap, in_ap, s1, s2, op0, op1=None, waits=()):
        if op1 is None:
            return self.p.add("vector",
                              lambda e, o=out_ap, i=in_ap, a=s1, p0=op0:
                              e.tensor_scalar(o, i, a, None, p0),
                              waits, ("dve", 1))
        return self.p.add("vector",
                          lambda e, o=out_ap, i=in_ap, a=s1, b=s2, p0=op0, p1=op1:
                          e.tensor_scalar(o, i, a, b, p0, p1),
                          waits, ("dve", 1))

    def tt(self, out_ap, a_ap, b_ap, op, waits=()):
        return self.p.add("vector",
                          lambda e, o=out_ap, a=a_ap, b=b_ap, p=op:
                          e.tensor_tensor(o, a, b, p), waits, ("dve", 1))

    def stt(self, out_ap, in0, scalar, in1, op0, op1, accum_out=None, waits=()):
        return self.p.add("vector",
                          lambda e, o=out_ap, a=in0, s=scalar, b=in1, p0=op0, p1=op1, ac=accum_out:
                          e.scalar_tensor_tensor(o, a, s, b, p0, p1, accum_out=ac),
                          waits, ("dve", 1))

    def copy(self, out_ap, in_ap, waits=()):
        return self.p.add("vector",
                          lambda e, o=out_ap, i=in_ap: e.tensor_copy(o, i),
                          waits, ("dve", 1))

    def reduce(self, out_ap, in_ap, op, absval=False, waits=()):
        return self.p.add("vector",
                          lambda e, o=out_ap, i=in_ap, p=op, a=absval:
                          e.tensor_reduce(o, i, AX, p, apply_absolute_value=a),
                          waits, ("dve", 1))

    def recip(self, out_ap, in_ap, waits=()):
        return self.p.add("vector",
                          lambda e, o=out_ap, i=in_ap: e.reciprocal(o, i),
                          waits, ("dve", 1))

    def memset(self, ap, val, waits=()):
        return self.p.add("vector",
                          lambda e, a=ap, v=val: e.memset(a, v),
                          waits, ("dve", 1))

    def barrier(self):
        toks = [(ss, self.p.counts[ss]) for ss in
                self.dma_sems + ["pe", "act", "dve"]]
        for eng in ("sync", "tensor", "scalar", "vector"):
            self.p.add(eng, None, waits=toks)

    def build(self):
        c, nc = self.c, self.nc
        S, D, INNER, DC = c.S, c.D, c.INNER, c.DC

        self.x_in = nc.dram_tensor("x", [c.SEQS, S, D], F32, kind="ExternalInput").ap()
        self.w_in = {}
        for n, shape in (("q", [D, D]), ("k", [D, D]), ("v", [D, D]), ("r", [D, D]),
                         ("g", [INNER, D]), ("u", [INNER, D]), ("d", [D, INNER])):
            self.w_in[n] = nc.dram_tensor("W" + n, shape, F32, kind="ExternalInput").ap()
        self.aux_in = nc.dram_tensor("aux", [128, 256], F32, kind="ExternalInput").ap()
        self.y_out = nc.dram_tensor("y", [c.SEQS, S, D], F32, kind="ExternalOutput").ap()

        self.wT_sc = {
            "g": self.dram("wgT_sc", [128, DC * INNER], BF16),
            "u": self.dram("wuT_sc", [128, DC * INNER], BF16),
            "d": self.dram("wdT_sc", [128, c.OC * D], BF16)}
        self.y1_sc = self.dram("y1_sc", [128, c.TC * D], F32)
        self.oth_sc = self.dram("oth_sc", [128, c.PAIRS * S], BF16)
        self.otl_sc = self.dram("otl_sc", [128, c.PAIRS * S], BF16)

        self.bank = [nc.alloc_psum_tensor(f"bank{i}", [128, 512], F32).ap()
                     for i in range(8)]
        self.bank_user = [None] * 8

        self.mask = self.sb("mask", [128, 128], F32)
        self.ident = self.sb("ident", [128, 128], F32)
        self.ones_bf = self.sb("ones_bf", [128, 128], BF16)
        self.ones_f = self.sb("ones_f", [128, 1], F32)
        self.ones_row = self.sb("ones_row", [1, 128], F32)
        self.negC = self.sb("negC", [128, 1], F32)
        self.negC50 = self.sb("negC50", [128, 1], F32)
        self.epsc = self.sb("epsc", [128, 1], F32)
        self.t_consts = [
            self.dma(self.mask, self.aux_in[:, 0:128]),
            self.dma(self.ident, self.aux_in[:, 128:256]),
            self.memset(self.ones_bf, 1.0),
            self.memset(self.ones_f, 1.0),
            self.memset(self.ones_row, 1.0),
            self.memset(self.negC, -C_STAB),
            self.memset(self.negC50, -C_STAB - 50.0),
            self.memset(self.epsc, 1e-5)]
        self.invsw = {}
        self.micols = {n: self.sb("micol_" + n, [128, 1], F32) for n in ("g", "u", "d")}
        self.wsplit_toks = {"q": [], "k": [], "v": []}
        self.wpre_tok = {}

        for s in range(c.SEQS):
            self.run_seq(s)
            self.barrier()
        self.p.add("sync", None,
                   waits=[(ss, self.p.counts[ss]) for ss in self.dma_sems])
        return self

    def pump_prep(self, n=1):
        g = getattr(self, "prep_gen", None)
        if g is None:
            return
        for _ in range(n):
            if next(g, "END") == "END":
                self.prep_gen = None
                return

    def prep_alloc(self, _es):
        c = self.c
        mx = max(c.D, min(c.INNER, 1024))
        al = lambda *a, **k: _es.enter_context(self.sbg(*a, **k)).ap()
        self.prep_bufs = dict(
            wbuf=[al(f"pw{i}", [128, mx], F32) for i in range(4)],
            acc=al("pacc", [128, 1], F32),
            acc2=al("pacc2", [128, 1], F32),
            tmp1=al("ptmp", [1, 4], F32),
            terf=al("pterf", [128, mx], F32),
            ter=al("pter", [128, mx], BF16),
            terT=al("pterT", [128, mx], BF16, is_transpose_dest=True),
            gcol=al("pgcol", [128, 1], F32))
        self.prep_state = {"t_use": [None] * 4, "t_gcol_free": None}

    def prep_weight_gen(self, wname):
        c = self.c
        D, INNER = c.D, c.INNER
        pb = self.prep_bufs
        wbuf, acc, acc2, tmp1 = pb["wbuf"], pb["acc"], pb["acc2"], pb["tmp1"]
        terf, ter, terT, gcol = pb["terf"], pb["ter"], pb["terT"], pb["gcol"]
        t_c = self.t_consts
        t_gcol_free = self.prep_state["t_gcol_free"]
        if True:
            for wname, rows, cols in ((wname, {"g": INNER, "u": INNER,
                                               "d": D}[wname],
                                       {"g": D, "u": D, "d": INNER}[wname]),):
                w_in, sc = self.w_in[wname], self.wT_sc[wname]
                nch = rows // 128
                t_use = self.prep_state["t_use"]
                t_acc = None
                ceff1 = min(cols, 1024)
                n1 = nch * (cols // ceff1)
                t_lds = {}

                def issue1(rr):
                    r, cf = divmod(rr, cols // ceff1)
                    b = rr % 4
                    t_lds[rr] = self.dma(wbuf[b][:, 0:ceff1],
                                         w_in[r * 128:(r + 1) * 128,
                                              cf * ceff1:(cf + 1) * ceff1],
                                         waits=[t_use[b]])
                for rr in range(min(3, n1)):
                    issue1(rr)
                for rr in range(n1):
                    if rr + 3 < n1:
                        issue1(rr + 3)
                    b = rr % 4
                    if rr == 0:
                        t_acc = self.reduce(acc, wbuf[b][:, 0:ceff1], ALU.add,
                                            absval=True, waits=[t_lds[rr]])
                    else:
                        t_r = self.reduce(acc2, wbuf[b][:, 0:ceff1], ALU.add,
                                          absval=True,
                                          waits=[t_lds[rr], t_acc])
                        t_acc = self.tt(acc, acc, acc2, ALU.add, waits=[t_r])
                    t_use[b] = t_acc
                    yield
                t_mm = self.mm(self.bank[0][0:1, 0:1], self.ones_f, acc, True,
                               True, waits=[t_acc, self.bank_user[0]] + t_c)
                t_mean = self.ts(tmp1[0:1, 0:1], self.bank[0][0:1, 0:1],
                                 1.0 / float(rows * cols), None, ALU.mult,
                                 waits=[t_mm])
                self.bank_user[0] = t_mean
                t_cl = self.ts(tmp1[0:1, 1:2], tmp1[0:1, 0:1], 1e-5, None,
                               ALU.max, waits=[t_mean])
                t_sw = self.recip(tmp1[0:1, 2:3], tmp1[0:1, 1:2], waits=[t_cl])
                micol = self.micols[wname]
                t_b1 = self.mm(self.bank[0][:, 0:1], self.ones_row,
                               tmp1[0:1, 2:3], True, True,
                               waits=[t_sw, self.bank_user[0]])
                t_g = self.copy(gcol, self.bank[0][:, 0:1],
                                waits=[t_b1, t_gcol_free])
                self.bank_user[0] = t_g
                t_b2 = self.mm(self.bank[0][:, 0:1], self.ones_row,
                               tmp1[0:1, 1:2], True, True,
                               waits=[t_cl, self.bank_user[0]])
                t_mi = self.copy(micol, self.bank[0][:, 0:1], waits=[t_b2])
                self.bank_user[0] = t_mi
                self.invsw[wname] = (micol, t_mi)
                yield

                t_terf_free = None
                t_ter_free = None
                t_terT_free = None
                ceff = min(cols, 1024)
                ncf = cols // ceff
                nchc = ceff // 128
                n2 = nch * ncf
                t_lds2 = {}

                def issue2(idx):
                    r, cf = divmod(idx, ncf)
                    b = idx % 4
                    t_lds2[idx] = self.dma(
                        wbuf[b][:, 0:ceff],
                        w_in[r * 128:(r + 1) * 128,
                             cf * ceff:(cf + 1) * ceff],
                        waits=[t_use[b]])
                for idx in range(min(3, n2)):
                    issue2(idx)
                for idx in range(n2):
                    r, cf = divmod(idx, ncf)
                    if True:
                        if idx + 3 < n2:
                            issue2(idx + 3)
                        b = idx % 4
                        t1 = self.ts(terf[:, 0:ceff], wbuf[b][:, 0:ceff],
                                     gcol, MAGIC, ALU.mult, ALU.add,
                                     waits=[t_lds2[idx], t_g, t_terf_free])
                        yield
                        t2 = self.ts(terf[:, 0:ceff], terf[:, 0:ceff], -MAGIC,
                                     1.0, ALU.add, ALU.min, waits=[t1])
                        yield
                        t3 = self.ts(ter[:, 0:ceff], terf[:, 0:ceff], -1.0,
                                     None, ALU.max, waits=[t2, t_ter_free])
                        t_use[b] = t3
                        t_terf_free = t3
                        t_gcol_free = t1
                        t_tr = self.dmat(
                            terT[:, 0:ceff].rearrange("p (ch q) -> p ch q",
                                                      ch=nchc),
                            ter[:, 0:ceff], waits=[t3, t_terT_free])
                        t_ter_free = t_tr
                        t_st = self.dma(
                            sc.rearrange("p (ch rr) -> p ch rr",
                                         ch=cols // 128)[
                                :, cf * nchc:(cf + 1) * nchc,
                                r * 128:(r + 1) * 128],
                            terT[:, 0:ceff].rearrange("p (ch q) -> p ch q",
                                                      ch=nchc),
                            waits=[t_tr])
                        t_terT_free = t_st
                        yield
            self.prep_state["t_gcol_free"] = t_gcol_free

    def run_seq(self, s):
        c = self.c
        S, D, DC = c.S, c.D, c.DC
        with ExitStack() as _es0:
            XT_hi = _es0.enter_context(self.sbg("XT_hi", [128, DC * S], BF16, is_transpose_dest=True)).ap()
            XT_lo = _es0.enter_context(self.sbg("XT_lo", [128, DC * S], BF16, is_transpose_dest=True)).ap()
            with ExitStack() as _es1:
                x_tm = _es1.enter_context(self.sbg("x_tm", [128, c.TC * D], F32, side="right")).ap()
                xhi = _es1.enter_context(self.sbg("xhi", [128, c.TC * D], BF16)).ap()
                xlo = _es1.enter_context(self.sbg("xlo", [128, c.TC * D], BF16)).ap()
                nx = max(1, c.TC // 4)
                t_x4 = [self.dma(
                    x_tm[:, i * 4 * D:(i + 1) * 4 * D]
                    .rearrange("p (tc d) -> p tc d", tc=4),
                    self.x_in[s][i * 512:(i + 1) * 512, :]
                    .rearrange("(tc p) d -> p tc d", p=128))
                    for i in range(nx)]
                t_hi = self.copy(xhi, x_tm, waits=t_x4)
                t_lo = self.stt(xlo, x_tm, 1.0, xhi, ALU.mult, ALU.subtract,
                                waits=[t_hi])
                t_Xh_l, t_Xl_l = [], []
                for g in range(max(1, c.TC // 4)):
                    cs = slice(g * 4 * D, min((g + 1) * 4, c.TC) * D)
                    t_Xh_l.append(self.dmat(
                        XT_hi[:, cs].rearrange("p (blk tl) -> p blk tl",
                                               tl=128),
                        xhi[:, cs], waits=[t_hi]))
                    t_Xl_l.append(self.dmat(
                        XT_lo[:, cs].rearrange("p (blk tl) -> p blk tl",
                                               tl=128),
                        xlo[:, cs], waits=[t_lo]))
            self.t_XT_parts = (t_Xh_l, t_Xl_l)

            wstage = _es0.enter_context(
                self.sbg("wstage", [128, 3 * D], F32)).ap()
            _al0 = lambda *a, **k: _es0.enter_context(self.sbg(*a, **k)).ap()
            nbufs = (_al0("rls", [128, 512], F32),
                     _al0("rlb", [128, 1024], BF16),
                     _al0("osr", [64, 512], F32),
                     _al0("onf", [64, 512], F32),
                     [_al0("ogh0", [64, 512], BF16),
                      _al0("ogh1", [64, 512], BF16)],
                     [_al0("ogl0", [64, 512], BF16),
                      _al0("ogl1", [64, 512], BF16)])
            self.prefetch_w(0, wstage)
            if s == 0:
                with ExitStack() as _esp:
                    self.prep_alloc(_esp)
                    def _gen():
                        for w in ("g", "u", "d"):
                            yield from self.prep_weight_gen(w)
                    self.prep_gen = _gen()
                    for pair in range(c.PAIRS):
                        self.attn_pair(s, pair, XT_hi, XT_lo, wstage, nbufs)
                    self.flush_norm()
                    self.pump_prep(10 ** 6)
            else:
                for pair in range(c.PAIRS):
                    self.attn_pair(s, pair, XT_hi, XT_lo, wstage, nbufs)
                self.flush_norm()
        self.barrier()

        with ExitStack() as _es2:
            XqT = _es2.enter_context(self.sbg("XqT", [128, DC * S], BF16, is_transpose_dest=True)).ap()
            cg_cols = _es2.enter_context(self.sbg("cg_cols", [128, c.TC], F32)).ap()
            cu_cols = _es2.enter_context(self.sbg("cu_cols", [128, c.TC], F32)).ap()
            self.wr_ln1(s, XqT, cg_cols, cu_cols)
            self.barrier()
            self.ffn(s, XqT, cg_cols, cu_cols)

    def flush_norm(self):
        if self.norm_pending is not None:
            self.norm_pending[0]()
            self.norm_pending[1]()
            self.norm_pending = None

    def t_XT_for(self, qp):
        h, l = self.t_XT_parts
        return [h[-1], l[-1]]

    def prefetch_w(self, pair, wstage):
        c = self.c
        for i, name in enumerate(("q", "k", "v")):
            src = self.w_in[name][:, pair * 128:(pair + 1) * 128] \
                .rearrange("(dh dl) j -> dl dh j", dl=128)
            self.wpre_tok[name] = self.dma(
                wstage.rearrange("p (n dh j) -> p n dh j", n=3, dh=c.DC)[:, i],
                src, waits=self.wsplit_toks[name])

    def attn_pair(self, s, pair, XT_hi, XT_lo, wstage, nbufs):
        c = self.c
        S, D, DC, QT, KC = c.S, c.D, c.DC, c.QT, c.KC
        with ExitStack() as _es:
            al = lambda *a, **k: _es.enter_context(self.sbg(*a, **k)).ap()

            wh = al("apwh", [128, D], BF16)
            wl = al("apwl", [128, D], BF16)
            qk = {"q": (al("qhi", [128, S], BF16), al("qlo", [128, S], BF16))}
            khp = [al("khp0", [128, S], BF16), al("khp1", [128, S], BF16)]
            khl = [al("khl0", [128, S], BF16), al("khl1", [128, S], BF16)]
            klp = [al("klp0", [128, S], BF16), al("klp1", [128, S], BF16)]

            vp = [al("vp0", [128, KC * 65], F32), al("vp1", [128, KC * 65], F32)]
            abuf = al("abuf", [128, KC * 512], F32)
            vts = abuf[:, 0:S]
            rls, rlb, osr, onf, ogh, ogl = nbufs

            def load_split(name):
                slot = {"q": 0, "k": 1, "v": 2}[name]
                src = wstage[:, slot * D:(slot + 1) * D]
                t_h = self.copy(wh, src, waits=[self.wpre_tok[name]]
                                + list(self.t_wsplit_done))
                t_l = self.stt(wl, src, 1.0, wh, ALU.mult, ALU.subtract,
                               waits=[t_h])
                self.wsplit_toks[name] = [t_h, t_l]
                return [t_h, t_l]

            XhR = XT_hi.rearrange("p (tcc dh tl) -> p tcc dh tl",
                                  tcc=S // 128, dh=DC)
            XlR = XT_lo.rearrange("p (tcc dh tl) -> p tcc dh tl",
                                  tcc=S // 128, dh=DC)

            def project(t_w, evac):
                toks = []
                grp = [None, None]
                for qp in range((QT + 1) // 2):
                    js = [j for j in (0, 1) if 2 * qp + j < QT]
                    grp = [None, None]
                    for ph, (wop, xop) in enumerate(((wh, XhR), (wh, XlR),
                                                     (wl, XhR))):
                        for dh in range(DC):
                            last = (ph == 2 and dh == DC - 1)
                            for j in js:
                                qt = 2 * qp + j
                                bk = (2 * (qp % 2) + j) if QT > 2 else j
                                w0 = (t_w + [self.bank_user[bk]]
                                      + self.t_XT_for(qp)) \
                                    if (ph == 0 and dh == 0) else ()
                                grp[j] = self.mm(
                                    self.bank[bk],
                                    wop[:, dh * 128:(dh + 1) * 128],
                                    xop[:, 4 * qt:4 * qt + 4, dh],
                                    ph == 0 and dh == 0, last, waits=w0)
                    for j in js:
                        bk = (2 * (qp % 2) + j) if QT > 2 else j
                        toks.append(evac(2 * qp + j, bk, grp[j]))
                return toks, [g for g in grp if g is not None]

            t_w = load_split("q")
            hi, lo = qk["q"]

            def evac_q(qt, bk, t_grp, hi=hi, lo=lo):
                th = self.copy(hi[:, qt * 512:(qt + 1) * 512], self.bank[bk],
                               waits=[t_grp])
                tl = self.stt(lo[:, qt * 512:(qt + 1) * 512], self.bank[bk],
                              1.0, hi[:, qt * 512:(qt + 1) * 512],
                              ALU.mult, ALU.subtract, waits=[th])
                self.bank_user[bk] = tl
                return tl
            t_qk, q_grps = project(t_w, evac_q)
            self.t_wsplit_done = q_grps

            t_w = load_split("k")
            t_kz = [self.memset(t, 0.0, waits=[self.last_score_tok])
                    for t in khp]

            def evac_k(qt, bk, t_grp):
                toks = []
                for h in (0, 1):
                    rs = slice(64 * h, 64 * h + 64)
                    cs = slice(qt * 512, (qt + 1) * 512)
                    th = self.copy(khp[h][rs, cs], self.bank[bk][rs, :],
                                   waits=[t_grp] + t_kz)
                    th2 = self.copy(khl[h][rs, cs], self.bank[bk][rs, :],
                                    waits=[t_grp, self.last_score_tok])
                    tl = self.stt(klp[h][rs, cs], self.bank[bk][rs, :], 1.0,
                                  khp[h][rs, cs], ALU.mult, ALU.subtract,
                                  waits=[th] + (self.t_qrep_done or []))
                    toks += [th2, tl]
                self.bank_user[bk] = toks[-1]
                return toks[-1]
            t_qk, k_grps = project(t_w, evac_k)
            self.t_wsplit_done = k_grps

            t_w = load_split("v")

            def evac_v(qt, bk, t_grp):
                tv = self.copy(vts[:, qt * 512:(qt + 1) * 512], self.bank[bk],
                               waits=[t_grp, self.ab_user.get(qt)])
                self.bank_user[bk] = tv
                return tv
            t_v, v_grps = project(t_w, evac_v)
            self.t_wsplit_done = v_grps
            if pair + 1 < c.PAIRS:
                self.prefetch_w(pair + 1, wstage)

            t_qrep = []
            for h in (0, 1):
                ro = slice(64 * (1 - h), 64 * (1 - h) + 64)
                rh = slice(64 * h, 64 * h + 64)
                t_sh = self.dma(khl[h][ro, :], klp[h][rh, :],
                                waits=[t_qk[-1], self.last_score_tok])
                t_q1 = self.copy(klp[h][rh, :], qk["q"][0][rh, :],
                                 waits=[t_sh, t_qk[-1]])
                t_q2 = self.dma(klp[h][ro, :], qk["q"][0][rh, :],
                                waits=[t_sh, t_qk[-1]])
                t_qrep += [t_sh, t_q1, t_q2]
            self.t_qrep_done = t_qrep
            qrep = klp

            t_ones = [self.memset(
                vp[h].rearrange("p (kc cc) -> p kc cc", cc=65)[:, :, 64:65],
                1.0, waits=[self.last_av_tok]) for h in (0, 1)]
            t_vp_last = None
            for kc in range(KC):
                bk_t = (6, 0)[kc % 2]
                t_t = self.tr(self.bank[bk_t][:, 0:128],
                              vts[:, kc * 128:(kc + 1) * 128],
                              waits=[t_v[(kc * 128) // 512],
                                     self.bank_user[bk_t], self.t_consts[1]])
                tc0 = self.copy(vp[0][:, kc * 65: kc * 65 + 64],
                                self.bank[bk_t][:, 0:64], waits=[t_t] + t_ones)
                tc1 = self.copy(vp[1][:, kc * 65: kc * 65 + 64],
                                self.bank[bk_t][:, 64:128], waits=[t_t])
                self.bank_user[bk_t] = tc1
                t_vp_last = tc1

            self.avslot = 1
            t_ot_last = None
            for h in (0, 1):
                hi_q, lo_q = qk["q"]
                for qt in range(QT):
                    avb = 2 + (self.avslot % 3)
                    self.avslot += 1
                    t_av_grp = None
                    kc_hi = min(KC, 4 * qt + 4)
                    if self.norm_pending is not None:
                        self.norm_pending[0]()
                        self.pump_prep(3)
                    t_exp = [None] * kc_hi
                    for kc in range(kc_hi):
                        bk = (0, 1, 7)[kc % 3]
                        w0 = [self.bank_user[bk], t_qk[-1], t_v[-1]] + \
                            self.t_qrep_done
                        t_s = self.mm(self.bank[bk],
                                      khl[h][:, kc * 128:(kc + 1) * 128],
                                      qrep[h][:, qt * 512:(qt + 1) * 512],
                                      True, False, waits=w0)
                        t_s = self.mm(self.bank[bk],
                                      khp[h][:, kc * 128:(kc + 1) * 128],
                                      lo_q[:, qt * 512:(qt + 1) * 512],
                                      False, True, waits=())
                        self.last_score_tok = t_s
                        a_off = kc * 128 - qt * 512
                        ab = abuf[:, kc * 512: kc * 512 + 512]
                        w_e = [t_s, self.ab_user.get(kc), t_vp_last]
                        t_el = []
                        if a_off + 128 <= 0:
                            t_el.append(self.act(ab, self.bank[bk], AF.Exp,
                                                 bias=self.negC, scale=0.125,
                                                 waits=w_e))
                        else:
                            t_m = self.tt(self.bank[bk][:, a_off:a_off + 128],
                                          self.bank[bk][:, a_off:a_off + 128],
                                          self.mask, ALU.add,
                                          waits=[t_s, self.t_consts[0]])
                            w_e = [t_m, self.ab_user.get(kc), t_vp_last]
                            if a_off > 0:
                                t_el.append(self.act(ab[:, 0:a_off],
                                                     self.bank[bk][:, 0:a_off],
                                                     AF.Exp, bias=self.negC50,
                                                     scale=0.125, waits=w_e))
                            t_el.append(self.act(ab[:, a_off:512],
                                                 self.bank[bk][:, a_off:512],
                                                 AF.Exp, bias=self.negC,
                                                 scale=0.125, waits=w_e))
                        self.bank_user[bk] = t_el[-1]
                        t_exp[kc] = t_el
                    if self.norm_pending is not None:
                        self.norm_pending[1]()
                        self.norm_pending = None
                    for kc in range(kc_hi):
                        w_av = list(t_exp[kc]) + [t_vp_last]
                        if kc == 0:
                            w_av.append(self.bank_user[avb])
                        t_av_grp = self.mm(self.bank[avb][0:65, :],
                                           vp[h][:, kc * 65:(kc + 1) * 65],
                                           abuf[:, kc * 512: kc * 512 + 512],
                                           kc == 0, kc == kc_hi - 1,
                                           waits=w_av, force_inc=True)
                        self.ab_user[kc] = t_av_grp
                    self.last_av_tok = t_av_grp

                    st_rl = [None, None]

                    def norm_pre(avb=avb, t_av_grp=t_av_grp, st_rl=st_rl):
                        st_rl[0], st_rl[1] = self.do_norm_pre(
                            avb, t_av_grp, rls, rlb)

                    def norm_rest(avb=avb, h=h, qt=qt, t_av_grp=t_av_grp,
                                  st_rl=st_rl):
                        self.do_normalize(pair, avb, h, qt, t_av_grp,
                                          st_rl[0], st_rl[1], rls, rlb,
                                          osr, onf, ogh, ogl)
                    self.norm_pending = (norm_pre, norm_rest)
            t_ot_last = self.t_ot_store
        self.t_ot_store = t_ot_last

    def do_norm_pre(self, avb, t_av_grp, rls, rlb):
        t_rl = self.recip(rls[64:65, :], self.bank[avb][64:65, :],
                          waits=[t_av_grp, self.nrm_prev])
        t_rh = self.copy(rlb[64:65, 0:512], rls[64:65, :],
                         waits=[t_rl])
        t_rw = self.stt(rlb[64:65, 512:1024], rls[64:65, :], 1.0,
                        rlb[64:65, 0:512], ALU.mult, ALU.subtract,
                        waits=[t_rh])
        return t_rh, t_rw

    def do_normalize(self, pair, avb, h, qt, t_av_grp, t_rh, t_rw, rls, rlb,
                     osr, onf, ogh, ogl):
        c = self.c
        S = c.S
        if True:
            if True:
                if True:
                    self.mm(self.bank[5][0:64, :], self.ones_bf[64:65, 0:64],
                            rlb[64:65, 0:512], True, False,
                            waits=[t_rh, t_rw, self.bank_user[5],
                                   self.t_consts[2]], inc=False)
                    t_b2 = self.mm(self.bank[5][0:64, :],
                                   self.ones_bf[64:65, 0:64],
                                   rlb[64:65, 512:1024], False, True, waits=())
                    t_or = self.copy(osr, self.bank[avb][0:64, :],
                                     waits=[t_av_grp, self.nrm_prev])
                    self.bank_user[avb] = t_or
                    t_on = self.tt(onf, osr, self.bank[5][0:64, :], ALU.mult,
                                   waits=[t_or, t_b2])
                    self.bank_user[5] = t_on
                    par = self.avslot % 2
                    t_oh = self.copy(ogh[par], onf,
                                     waits=[t_on] + (self.og_store[par] or []))
                    t_ol = self.stt(ogl[par], onf, 1.0, ogh[par], ALU.mult,
                                    ALU.subtract, waits=[t_oh])
                    rs = slice(64 * h, 64 * h + 64)
                    cs = slice(pair * S + qt * 512, pair * S + (qt + 1) * 512)
                    t_s1 = self.dma(self.oth_sc[rs, cs], ogh[par],
                                    waits=[t_ol])
                    t_s2 = self.dma(self.otl_sc[rs, cs], ogl[par],
                                    waits=[t_ol])
                    self.og_store[par] = [t_s1, t_s2]
                    self.nrm_prev = t_ol
                    self.t_ot_store = t_s2
                    self.pump_prep(4)

    def wr_ln1(self, s, XqT, cg_cols, cu_cols):
        c = self.c
        S, D, DC, JT = c.S, c.D, c.DC, c.JT
        with ExitStack() as _es:
            al = lambda *a, **k: _es.enter_context(self.sbg(*a, **k)).ap()
            wrf = al("wrf", [128, DC * D], F32)
            wrh = al("wrh", [128, DC * D], BF16)
            wrl = al("wrl", [128, DC * D], BF16)
            othb = al("oth_b", [128, 4 * DC * 128], BF16)
            otlb = al("otl_b", [128, 4 * DC * 128], BF16)
            xrd = al("xrd", [128, 4 * D], F32)
            zbuf = al("zbuf", [128, D], F32)
            zsq = al("zsq", [128, D], F32)
            y1t = al("y1t", [128, D], F32)
            ubuf = al("ubuf", [128, D], F32)
            xqtm = al("xqtm", [128, c.TC * D], BF16)
            st = al("stat", [128, 12], F32)

            t_wr = self.dma(wrf.rearrange("p (dvc j) -> p dvc j", dvc=DC),
                            self.w_in["r"].rearrange("(dvc dl) j -> dl dvc j",
                                                     dl=128),
                            waits=list(self.t_wsplit_done))
            t_wh = self.copy(wrh, wrf, waits=[t_wr])
            t_wl = self.stt(wrl, wrf, 1.0, wrh, ALU.mult, ALU.subtract,
                            waits=[t_wh])

            t_slab_use = [None] * 4
            t_x_use = [None] * 4
            prev_apply = None
            prev_u = None
            prev_y1_store = None
            prev_ms = None
            t_xq_list = []
            lds = {}

            def load_tc(tc):
                b = tc % 4
                osl = slice(b * DC * 128, (b + 1) * DC * 128)
                src_h = self.oth_sc.rearrange(
                    "p (pr t) -> p pr t", pr=c.PAIRS)[
                    :, :, tc * 128:(tc + 1) * 128]
                src_l = self.otl_sc.rearrange(
                    "p (pr t) -> p pr t", pr=c.PAIRS)[
                    :, :, tc * 128:(tc + 1) * 128]
                w_sl = [t_slab_use[b], self.t_ot_store]
                t_s1 = self.dma(othb[:, osl].rearrange(
                    "p (pr t) -> p pr t", pr=DC), src_h, waits=w_sl)
                t_s2 = self.dma(otlb[:, osl].rearrange(
                    "p (pr t) -> p pr t", pr=DC), src_l, waits=w_sl)
                t_xl = self.dma(xrd[:, b * D:(b + 1) * D],
                                self.x_in[s, tc * 128:(tc + 1) * 128, :],
                                waits=[t_x_use[b]])
                lds[tc] = (t_s1, t_s2, t_xl)
            for tc in range(min(3, c.TC)):
                load_tc(tc)
            for tc in range(c.TC):
                if tc + 3 < c.TC:
                    load_tc(tc + 3)
                b = tc % 4
                t_s1, t_s2, t_xl = lds[tc]
                t_zs = []
                t_grp_last = None
                grp = [None] * JT
                for ph, (oop, wop) in enumerate(((othb, wrh), (othb, wrl),
                                                 (otlb, wrh))):
                    for dvc in range(DC):
                        last = (ph == 2 and dvc == DC - 1)
                        for jt in range(JT):
                            bk = 4 + 2 * (tc % 2) + jt
                            w0 = ([t_s1, t_s2, t_wl, self.bank_user[bk]]
                                  if (ph == 0 and dvc == 0) else ())
                            grp[jt] = self.mm(
                                self.bank[bk],
                                oop[:, b * DC * 128 + dvc * 128:
                                    b * DC * 128 + dvc * 128 + 128],
                                wop[:, dvc * D + jt * 512:
                                    dvc * D + jt * 512 + 512],
                                ph == 0 and dvc == 0, last, waits=w0)
                for jt in range(JT):
                    bk = 4 + 2 * (tc % 2) + jt
                    t_grp = grp[jt]
                    t_grp_last = t_grp
                    t_z = self.stt(zbuf[:, jt * 512:(jt + 1) * 512],
                                   self.bank[bk], 1.0,
                                   xrd[:, b * D + jt * 512:
                                       b * D + jt * 512 + 512],
                                   ALU.mult, ALU.add,
                                   accum_out=st[:, jt:jt + 1],
                                   waits=[t_grp, t_xl, prev_apply])
                    self.bank_user[bk] = t_z
                    t_zs.append(t_z)
                t_slab_use[b] = t_grp_last
                t_x_use[b] = t_zs[-1]
                if JT > 1:
                    t_sum = self.tt(st[:, 2:3], st[:, 0:1], st[:, 1:2], ALU.add,
                                    waits=t_zs)
                else:
                    t_sum = self.copy(st[:, 2:3], st[:, 0:1], waits=t_zs)
                t_sq = self.act(zsq, zbuf, AF.Square, accum_out=st[:, 3:4],
                                waits=t_zs + [prev_ms])
                t_mean = self.ts(st[:, 4:5], st[:, 2:3], 1.0 / D, None,
                                 ALU.mult, waits=[t_sum])
                t_ms = self.ts(st[:, 5:6], st[:, 3:4], 1.0 / D, None, ALU.mult,
                               waits=[t_sq])
                prev_ms = t_ms
                t_m2 = self.ts(st[:, 6:7], st[:, 4:5], st[:, 4:5], None,
                               ALU.mult, waits=[t_mean])
                t_var = self.tt(st[:, 7:8], st[:, 5:6], st[:, 6:7],
                                ALU.subtract, waits=[t_ms, t_m2])
                t_sd = self.act(st[:, 8:9], st[:, 7:8], AF.Sqrt, bias=self.epsc,
                                waits=[t_var, self.t_consts[-1]])
                t_rs = self.recip(st[:, 9:10], st[:, 8:9],
                                  waits=[t_sd, prev_apply])
                t_lb = self.stt(st[:, 10:11], st[:, 4:5], -1.0, st[:, 9:10],
                                ALU.mult, ALU.mult, waits=[t_rs, prev_apply])
                t_y1 = self.ts(y1t, zbuf, st[:, 9:10], st[:, 10:11],
                               ALU.mult, ALU.add,
                               waits=[t_lb, prev_u, prev_y1_store])
                prev_apply = t_y1
                t_st1 = self.dma(self.y1_sc[:, tc * D:(tc + 1) * D], y1t,
                                 waits=[t_y1])
                prev_y1_store = t_st1
                t_am = self.reduce(st[:, 11:12], y1t, ALU.max, absval=True,
                                   waits=[t_y1])
                t_ac = self.ts(st[:, 0:1], st[:, 11:12], 1e-5, None, ALU.max,
                               waits=[t_am])
                t_rc = self.recip(st[:, 1:2], st[:, 0:1], waits=[t_ac])
                t_sx = self.ts(st[:, 2:3], st[:, 1:2], 127.0, None, ALU.mult,
                               waits=[t_rc])
                micol, t_mi = self.invsw["g"]
                t_cg = self.stt(cg_cols[:, tc:tc + 1], st[:, 0:1], 1.0 / 127.0,
                                micol, ALU.mult, ALU.mult, waits=[t_ac, t_mi])
                micol_u, t_miu = self.invsw["u"]
                t_cu = self.stt(cu_cols[:, tc:tc + 1], st[:, 0:1], 1.0 / 127.0,
                                micol_u, ALU.mult, ALU.mult,
                                waits=[t_ac, t_miu])

                t_u = self.ts(ubuf, y1t, st[:, 2:3], MAGIC, ALU.mult, ALU.add,
                              waits=[t_sx])
                prev_u = t_u
                t_xq = self.ts(xqtm[:, tc * D:(tc + 1) * D], ubuf, -MAGIC, None,
                               ALU.add, waits=[t_u])
                t_xq_list.append(t_xq)
                self.t_cg_done = t_cg
                self.t_cu_done = t_cu
            self.t_xq_parts = []
            for g in range(c.TC // c.TSB):
                cs = slice(g * c.TSB * D, (g + 1) * c.TSB * D)
                self.t_xq_parts.append(self.dmat(
                    XqT[:, cs].rearrange("p (blk tl) -> p blk tl", tl=128),
                    xqtm[:, cs],
                    waits=[t_xq_list[tc]
                           for tc in range(g * c.TSB, (g + 1) * c.TSB)]))
            self.t_xq_done = self.t_xq_parts[-1]
            if self.c.debug:
                dbg1 = self.dram("dbg_XqT", [128, DC * S], BF16)
                dbg2 = self.dram("dbg_cg", [128, c.TC], F32)
                self.dma(dbg1, XqT, waits=[self.t_xq_done])
                self.dma(dbg2, cg_cols, waits=[self.t_cg_done])

    def ffn(self, s, XqT, cg_cols, cu_cols):
        c = self.c
        S, D, DC, JT, INNER = c.S, c.D, c.DC, c.JT, c.INNER
        OC, OTILE, TSB = c.OC, c.OTILE, c.TSB
        HB = TSB // 2 or 1
        NH = TSB // HB
        WOC = 2
        with ExitStack() as _es:
            al = lambda *a, **k: _es.enter_context(self.sbg(*a, **k)).ap()
            htsb = al("htsb", [128, TSB * INNER], F32)
            hq = al("hq", [128, 2 * INNER], BF16)
            hqT = al("hqT", [128, TSB * OC * 128], BF16, is_transpose_dest=True)
            wgsb = al("wgsb", [128, 2 * DC * 512], BF16)
            wusb = al("wusb", [128, 2 * DC * 512], BF16)
            wdsb = al("wdsb", [128, 2 * WOC * D], BF16)
            gts = al("gts", [128, 2 * 512], F32)
            z2 = [al("z2a", [128, D], F32), al("z2b", [128, D], F32)]
            sqb = al("sqb", [128, D], F32)
            cdc = al("cdc", [128, 2 * TSB], F32)
            pmax = al("pmax", [128, TSB * OTILE], F32)
            st = al("stat2", [128, 28], F32)
            micol_d, t_mid = self.invsw["d"]

            state = dict(
                t_g_use=[None, None], t_u_use=[None, None],
                t_d_use=[None, None], ep_i=0,
                gts_user=[None, None],
                t_hq_free=[None, None],
                t_hqT_use=[None] * TSB,
                t_h_free=[None] * TSB,
                t_cd_use=[None] * (2 * TSB),
                z2_users=[[], []],
                pm_use=[None] * TSB,
                prev_qtok=None)

            def load_a(ot):
                b = ot % 2
                wg_s = wgsb[:, b * DC * 512:(b + 1) * DC * 512]
                wu_s = wusb[:, b * DC * 512:(b + 1) * DC * 512]
                t_lg = self.dma(
                    wg_s.rearrange("p (dh o) -> p dh o", dh=DC),
                    self.wT_sc["g"].rearrange("p (dh o) -> p dh o", dh=DC)[
                        :, :, ot * 512:(ot + 1) * 512],
                    waits=[state["t_g_use"][b]])
                t_lu = self.dma(
                    wu_s.rearrange("p (dh o) -> p dh o", dh=DC),
                    self.wT_sc["u"].rearrange("p (dh o) -> p dh o", dh=DC)[
                        :, :, ot * 512:(ot + 1) * 512],
                    waits=[state["t_u_use"][b]])
                return t_lg, t_lu

            def phase_a(tsb, pre=None):
                t_h_parts = [[] for _ in range(TSB)]
                t_pm_parts = [[] for _ in range(TSB)]
                for ot in range(OTILE):
                    b = ot % 2
                    wg_s = wgsb[:, b * DC * 512:(b + 1) * DC * 512]
                    wu_s = wusb[:, b * DC * 512:(b + 1) * DC * 512]
                    if pre is not None and ot < len(pre):
                        t_lg, t_lu = pre[ot]
                    else:
                        t_lg, t_lu = load_a(ot)
                    for ti in range(TSB):
                        tc = tsb * TSB + ti
                        hsl = htsb[:, ti * INNER:(ti + 1) * INNER]
                        bg, bu = 2 * b, 2 * b + 1
                        w0g = [t_lg, self.bank_user[bg],
                               self.t_xq_parts[tsb]]
                        w0u = [t_lu, self.bank_user[bu],
                               self.t_xq_parts[tsb]]
                        tg_grp = tu_grp = None
                        for dh in range(DC):
                            tg_grp = self.mm(
                                self.bank[bg],
                                XqT[:, tc * D + dh * 128:
                                    tc * D + dh * 128 + 128],
                                wg_s[:, dh * 512:(dh + 1) * 512],
                                dh == 0, dh == DC - 1,
                                waits=w0g if dh == 0 else ())
                        for dh in range(DC):
                            tu_grp = self.mm(
                                self.bank[bu],
                                XqT[:, tc * D + dh * 128:
                                    tc * D + dh * 128 + 128],
                                wu_s[:, dh * 512:(dh + 1) * 512],
                                dh == 0, dh == DC - 1,
                                waits=w0u if dh == 0 else ())
                        gsl = gts[:, b * 512:(b + 1) * 512]
                        t_si = self.act(gsl, self.bank[bg], AF.Silu,
                                        scale=cg_cols[:, tc:tc + 1],
                                        waits=[tg_grp, self.t_cg_done,
                                               state["gts_user"][b]])
                        self.bank_user[bg] = t_si
                        t_hc = self.tt(hsl[:, ot * 512:(ot + 1) * 512],
                                       self.bank[bu], gsl, ALU.mult,
                                       waits=[t_si, tu_grp,
                                              state["t_h_free"][ti]])
                        self.bank_user[bu] = t_hc
                        state["gts_user"][b] = t_hc
                        t_h_parts[ti].append(t_hc)
                        t_pm_parts[ti].append(self.reduce(
                            pmax[:, ti * OTILE + ot:ti * OTILE + ot + 1],
                            hsl[:, ot * 512:(ot + 1) * 512], ALU.max,
                            absval=True,
                            waits=[t_hc, state["pm_use"][ti]]))
                    state["t_g_use"][b] = t_h_parts[TSB - 1][-1]
                    state["t_u_use"][b] = t_h_parts[TSB - 1][-1]
                return t_h_parts, t_pm_parts

            def quant(tsb, hparts, tis, tr_out=None):
                t_h_parts, t_pm_parts = hparts
                off = (tsb % 2) * TSB
                for ti in tis:
                    tc = tsb * TSB + ti
                    hs = ti % 2
                    hsl = htsb[:, ti * INNER:(ti + 1) * INNER]
                    t_am = self.reduce(
                        st[:, 0:1], pmax[:, ti * OTILE:(ti + 1) * OTILE],
                        ALU.max, absval=True,
                        waits=t_pm_parts[ti] + [state["prev_qtok"]])
                    state["pm_use"][ti] = t_am
                    t_ac = self.ts(st[:, 1:2], st[:, 0:1], 1e-5, None, ALU.max,
                                   waits=[t_am])
                    t_rc = self.recip(st[:, 2:3], st[:, 1:2], waits=[t_ac])
                    t_sh = self.ts(st[:, 3:4], st[:, 2:3], 127.0, None,
                                   ALU.mult, waits=[t_rc])
                    t_cd0 = self.stt(st[:, 15:16], st[:, 1:2], 1.0 / 127.0,
                                     cu_cols[:, tc:tc + 1], ALU.mult, ALU.mult,
                                     waits=[t_ac, self.t_cu_done,
                                            state["t_cd_use"][off + ti]])
                    t_cd = self.tt(cdc[:, off + ti:off + ti + 1],
                                   st[:, 15:16], micol_d,
                                   ALU.mult, waits=[t_cd0, t_mid])
                    state["t_cd_use"][off + ti] = t_cd
                    t_u2 = self.ts(hsl, hsl, st[:, 3:4], MAGIC, ALU.mult,
                                   ALU.add, waits=[t_sh])
                    t_hq = self.ts(hq[:, hs * INNER:(hs + 1) * INNER], hsl,
                                   -MAGIC, None, ALU.add,
                                   waits=[t_u2, state["t_hq_free"][hs]])
                    state["t_h_free"][ti] = t_hq
                    state["prev_qtok"] = t_cd
                    if tr_out is None:
                        quant_tr([(ti, hs, t_hq)])
                    else:
                        tr_out.append((ti, hs, t_hq))

            def quant_tr(trs):
                for ti, hs, t_hq in trs:
                    t_tp = self.dmat(
                        hqT[:, ti * OC * 128:(ti + 1) * OC * 128]
                        .rearrange("p (oh q) -> p oh q", oh=OC),
                        hq[:, hs * INNER:(hs + 1) * INNER],
                        waits=[t_hq, state["t_hqT_use"][ti]])
                    state["t_hq_free"][hs] = t_tp
                    state["t_hqT_use"][ti] = t_tp

            y1_halves = ([(0, 512), (512, D)] if D > 512 else [(0, D)])

            def load_y1(tsb, half, adv):
                res = {}
                for idx, ti in enumerate(range(half * HB, (half + 1) * HB)):
                    par = (state["ep_i"] + adv + idx) % 2
                    tc = tsb * TSB + ti
                    w = state["z2_users"][par]
                    res[ti] = [self.dma(z2[par][:, a:b],
                                        self.y1_sc[:, tc * D + a:tc * D + b],
                                        waits=w)
                               for a, b in y1_halves]
                return res

            def load_b(ow):
                b = ow % 2
                return self.dma(
                    wdsb[:, b * WOC * D:(b + 1) * WOC * D],
                    self.wT_sc["d"][:, ow * WOC * D:(ow + 1) * WOC * D],
                    waits=[state["t_d_use"][b]])

            def phase_b(tsb, half, pre_ld=None, pre_y=None):
                cdo = (tsb % 2) * TSB
                tis = range(half * HB, (half + 1) * HB)
                t_dn_grp = {ti: [None] * JT for ti in tis}
                t_hqT_tp = {ti: state["t_hqT_use"][ti] for ti in tis}
                for ow in range(OC // WOC):
                    b = ow % 2
                    if pre_ld is not None and ow < len(pre_ld):
                        t_ld = pre_ld[ow]
                    else:
                        t_ld = load_b(ow)
                    for oi in range(WOC):
                        oc = ow * WOC + oi
                        for k, ti in enumerate(tis):
                            for jt in range(JT):
                                bk = 4 + k * JT + jt
                                w0 = [t_ld, t_hqT_tp[ti]]
                                if oc == 0:
                                    w0.append(self.bank_user[bk])
                                lastmm = (k == HB - 1 and jt == JT - 1)
                                wmm = w0 if oc == 0 else (
                                    [t_ld] if (oi == 0 and k == 0 and jt == 0)
                                    else ())
                                t_dn_grp[ti][jt] = self.mm(
                                    self.bank[bk],
                                    hqT[:, ti * OC * 128 + oc * 128:
                                        ti * OC * 128 + oc * 128 + 128],
                                    wdsb[:, b * WOC * D + oi * D + jt * 512:
                                         b * WOC * D + oi * D + jt * 512 + 512],
                                    oc == 0, oc == OC - 1,
                                    waits=wmm,
                                    force_inc=(lastmm and oi == WOC - 1))
                    state["t_d_use"][b] = t_dn_grp[tis[-1] if False else list(tis)[-1]][JT - 1]
                for k, ti in enumerate(tis):
                    state["t_hqT_use"][ti] = t_dn_grp[ti][JT - 1]
                    tc = tsb * TSB + ti
                    par = state["ep_i"] % 2
                    state["ep_i"] += 1
                    po = 12 * par
                    z2p = z2[par]
                    sqp = sqb
                    if pre_y is not None and ti in pre_y:
                        t_yls = pre_y[ti]
                    else:
                        t_yls = [self.dma(
                            z2p[:, a:b],
                            self.y1_sc[:, tc * D + a:tc * D + b],
                            waits=state["z2_users"][par])
                            for a, b in y1_halves]
                    t_zs = []
                    for jt in range(JT):
                        bk = 4 + k * JT + jt
                        t_z = self.stt(z2p[:, jt * 512:(jt + 1) * 512],
                                       self.bank[bk],
                                       cdc[:, cdo + ti:cdo + ti + 1],
                                       z2p[:, jt * 512: jt * 512 + 512],
                                       ALU.mult, ALU.add,
                                       accum_out=st[:, 4 + po + jt:
                                                    5 + po + jt],
                                       waits=[t_dn_grp[ti][jt]] + t_yls +
                                       [state["t_cd_use"][cdo + ti]])
                        self.bank_user[bk] = t_z
                        t_zs.append(t_z)
                    if JT > 1:
                        t_sum = self.tt(st[:, 6 + po:7 + po],
                                        st[:, 4 + po:5 + po],
                                        st[:, 5 + po:6 + po],
                                        ALU.add, waits=t_zs)
                    else:
                        t_sum = self.copy(st[:, 6 + po:7 + po],
                                          st[:, 4 + po:5 + po], waits=t_zs)
                    t_sq = self.act(sqp, z2p, AF.Square,
                                    accum_out=st[:, 7 + po:8 + po],
                                    waits=t_zs)
                    t_mean = self.ts(st[:, 8 + po:9 + po],
                                     st[:, 6 + po:7 + po], 1.0 / D, None,
                                     ALU.mult, waits=[t_sum])
                    t_ms = self.ts(st[:, 9 + po:10 + po],
                                   st[:, 7 + po:8 + po], 1.0 / D, None,
                                   ALU.mult, waits=[t_sq])
                    t_m2 = self.ts(st[:, 10 + po:11 + po],
                                   st[:, 8 + po:9 + po],
                                   st[:, 8 + po:9 + po], None,
                                   ALU.mult, waits=[t_mean])
                    t_var = self.tt(st[:, 11 + po:12 + po],
                                    st[:, 9 + po:10 + po],
                                    st[:, 10 + po:11 + po],
                                    ALU.subtract, waits=[t_ms, t_m2])
                    t_sd = self.act(st[:, 12 + po:13 + po],
                                    st[:, 11 + po:12 + po], AF.Sqrt,
                                    bias=self.epsc, waits=[t_var])
                    t_rs = self.recip(st[:, 13 + po:14 + po],
                                      st[:, 12 + po:13 + po], waits=[t_sd])
                    t_lb = self.stt(st[:, 14 + po:15 + po],
                                    st[:, 8 + po:9 + po], -1.0,
                                    st[:, 13 + po:14 + po],
                                    ALU.mult, ALU.mult, waits=[t_rs])
                    t_y2 = self.ts(z2p, z2p, st[:, 13 + po:14 + po],
                                   st[:, 14 + po:15 + po],
                                   ALU.mult, ALU.add, waits=[t_lb, t_sq])
                    t_st = self.dma(
                        self.y_out[s, tc * 128:(tc + 1) * 128, :], z2p,
                        waits=[t_y2])
                    state["z2_users"][par] = [t_sq, t_y2, t_st]

            pend = None
            pre = None
            for tsb in range(c.NTSB):
                pre_b = [load_b(0), load_b(1)] if pend is not None else None
                pre_y = load_y1(*pend, 0) if pend is not None else None
                hp = phase_a(tsb, pre)
                quant(tsb, hp, range(HB))
                tr23 = []
                quant(tsb, hp, range(HB, TSB), tr_out=tr23)
                if pend is not None:
                    phase_b(*pend, pre_ld=pre_b, pre_y=pre_y)
                pre_b0 = [load_b(0), load_b(1)]
                quant_tr(tr23)
                phase_b(tsb, 0, pre_ld=pre_b0)
                pre = ([load_a(0), load_a(1)]
                       if tsb + 1 < c.NTSB else None)
                pend = (tsb, 1)
            phase_b(*pend, pre_ld=[load_b(0), load_b(1)])

def _make_aux():
    aux = np.zeros((128, 256), np.float32)
    p = np.arange(128)[:, None]
    j = np.arange(128)[None, :]
    aux[:, 0:128] = np.where(p > j, np.float32(-400.0), np.float32(0.0))
    aux[:, 128:256] = np.eye(128, dtype=np.float32)
    return aux


def build_program(cfg=None):
    cfg = cfg or CFG()
    nc = bass.Bass("TRN2", target_bir_lowering=False, debug=False)
    b = Builder(nc, cfg)
    b.build()
    b.p.emit()
    return nc, b


def kernel(L0_attn_x, L0_attn_Wq, L0_attn_Wk, L0_attn_Wv, L0_attn_Wr,
           L0_ffn_w_gate, L0_ffn_w_up, L0_ffn_w_down):
    cfg = CFG()
    nc, _ = build_program(cfg)
    x = np.ascontiguousarray(np.asarray(L0_attn_x, np.float32))
    base = dict(Wq=np.ascontiguousarray(np.asarray(L0_attn_Wq, np.float32)),
                Wk=np.ascontiguousarray(np.asarray(L0_attn_Wk, np.float32)),
                Wv=np.ascontiguousarray(np.asarray(L0_attn_Wv, np.float32)),
                Wr=np.ascontiguousarray(np.asarray(L0_attn_Wr, np.float32)),
                Wg=np.ascontiguousarray(np.asarray(L0_ffn_w_gate, np.float32)),
                Wu=np.ascontiguousarray(np.asarray(L0_ffn_w_up, np.float32)),
                Wd=np.ascontiguousarray(np.asarray(L0_ffn_w_down, np.float32)),
                aux=_make_aux())
    in_maps = []
    for core in range(8):
        m = dict(base)
        m["x"] = np.ascontiguousarray(x[2 * core: 2 * core + 2])
        in_maps.append(m)
    res = run_bass_kernel_spmd(nc, in_maps, core_ids=list(range(8)))
    out = np.concatenate([np.asarray(r["y"], np.float32)
                          for r in res.results], axis=0)
    return out.reshape(16, 2048, 1024)


# version 50 (final)
# speedup vs baseline: 1.0235x; 1.0235x over previous
import sys
from contextlib import ExitStack
import numpy as np
import ml_dtypes
import concourse.bass as bass
import concourse.mybir as mybir
from concourse.bass_utils import run_bass_kernel_spmd

F32 = mybir.dt.float32
BF16 = mybir.dt.bfloat16
AF = mybir.ActivationFunctionType
ALU = mybir.AluOpType
AX = mybir.AxisListType.X
MAGIC = float(np.float32(3 * 2.0 ** 22))
C_STAB = 16.0


class CFG:
    def __init__(self, S=2048, D=1024, H=16, HD=64, INNER=4096, SEQS=2,
                 debug=False):
        self.debug = debug
        assert HD == 64
        self.S, self.D, self.H, self.HD, self.INNER, self.SEQS = S, D, H, HD, INNER, SEQS
        self.TC = S // 128
        self.QT = S // 512
        self.KC = S // 128
        self.PAIRS = H // 2
        self.DC = D // 128
        self.JT = D // 512
        self.OC = INNER // 128
        self.OTILE = INNER // 512
        self.TSB = min(4, self.TC)
        self.NTSB = self.TC // self.TSB
        assert self.PAIRS == self.DC


class Plan:
    ENGINES = ("sync", "tensor", "scalar", "vector")

    def __init__(self, nc):
        self.nc = nc
        self.ops = {e: [] for e in self.ENGINES}
        self.sems = {}
        self.counts = {}

    def sem(self, name):
        if name not in self.sems:
            self.sems[name] = self.nc.alloc_semaphore(name="sem_" + name)
            self.counts[name] = 0
        return name

    def add(self, eng, emit, waits=(), inc=None):
        red = {}
        for w in waits:
            if w is None:
                continue
            s, v = w
            if red.get(s, -1) < v:
                red[s] = v
        tok = None
        if inc is not None:
            s, k = inc
            self.sem(s)
            self.counts[s] += k
            tok = (s, self.counts[s])
        for s in red:
            self.sem(s)
        fr = sys._getframe(2)
        self.ops[eng].append((emit, tuple(red.items()), inc,
                              f"{fr.f_code.co_name}:{fr.f_lineno}"))
        return tok

    def emit(self):
        plans = self
        with self.nc.Block() as blk:
            def make_body(eng_name):
                def body(eng):
                    last = {}
                    for emit, waits, inc, tag in plans.ops[eng_name]:
                        for s, v in waits:
                            if last.get(s, -1) >= v:
                                continue
                            eng.wait_ge(plans.sems[s], v)
                            last[s] = v
                        try:
                            instr = emit(eng) if emit is not None else None
                        except Exception:
                            print(f"EMIT FAILED at {tag} on {eng_name}")
                            raise
                        if inc is not None:
                            s, k = inc
                            instr.then_inc(plans.sems[s], k)
                return body
            for e in self.ENGINES:
                if self.ops[e]:
                    getattr(blk, e)(make_body(e))


class Builder:
    NDMA = 8

    def __init__(self, nc, cfg: CFG):
        self.nc, self.c = nc, cfg
        self.p = Plan(nc)
        self.dma_sems = [f"dma{i}" for i in range(self.NDMA)]
        for s in self.dma_sems + ["pe", "act", "dve"]:
            self.p.sem(s)
        self._dma_rot = 0
        self._names = {}
        self.dma_floor = None
        self.t_wsplit_done = []
        self.t_qrep_done = None
        self.ab_user = {}
        self.og_store = [None, None]
        self.nrm_prev = None
        self.last_score_tok = None
        self.last_av_tok = None
        self.avslot = 0
        self.norm_pending = None
        self.t_ot_store = None

    def nm(self, base):
        i = self._names.get(base, 0)
        self._names[base] = i + 1
        return f"{base}_{i}" if i else base

    def sb(self, name, shape, dtype, **kw):
        return self.nc.alloc_sbuf_tensor(self.nm(name), shape, dtype, **kw).ap()

    def sbg(self, name, shape, dtype, **kw):
        return self.nc.sbuf_tensor(self.nm(name), shape, dtype, **kw)

    def dram(self, name, shape, dtype, kind="Internal"):
        return self.nc.dram_tensor(self.nm(name), shape, dtype, kind=kind).ap()

    def dma(self, out_ap, in_ap, waits=()):
        sem = self.dma_sems[self._dma_rot]
        self._dma_rot = (self._dma_rot + 1) % self.NDMA
        w = list(waits) + [(sem, self.p.counts[sem])]
        if self.dma_floor is not None:
            w.append(self.dma_floor)
        return self.p.add("sync", lambda e, o=out_ap, i=in_ap:
                          e.dma_start(out=o, in_=i), w, (sem, 16))

    def dmat(self, out_ap, in_ap, waits=()):
        sem = self.dma_sems[self._dma_rot]
        self._dma_rot = (self._dma_rot + 1) % self.NDMA
        fence = [(ss, self.p.counts[ss]) for ss in self.dma_sems]
        tok = self.p.add("sync", lambda e, o=out_ap, i=in_ap:
                         e.dma_start_transpose(o, i),
                         list(waits) + fence, (sem, 16))
        self.dma_floor = tok
        return tok

    def mm(self, out_ap, lhsT, rhs, start, stop, waits=(), inc=True,
           force_inc=False):
        do_inc = force_inc or (inc and stop)
        return self.p.add("tensor",
                          lambda e, o=out_ap, l=lhsT, r=rhs, a=start, b=stop:
                          e.matmul(o, l, r, start=a, stop=b),
                          waits, ("pe", 1) if do_inc else None)

    def tr(self, out_psum, in_sbuf, waits=()):
        return self.p.add("tensor",
                          lambda e, o=out_psum, i=in_sbuf, d=self.ident:
                          e.transpose(o, i, d), waits, ("pe", 1))

    def act(self, out_ap, in_ap, func, bias=0.0, scale=1.0, accum_out=None, waits=()):
        return self.p.add("scalar",
                          lambda e, o=out_ap, i=in_ap, f=func, b=bias, s=scale, a=accum_out:
                          e.activation(o, i, f, bias=b, scale=s, accum_out=a),
                          waits, ("act", 1))

    def ts(self, out_ap, in_ap, s1, s2, op0, op1=None, waits=()):
        if op1 is None:
            return self.p.add("vector",
                              lambda e, o=out_ap, i=in_ap, a=s1, p0=op0:
                              e.tensor_scalar(o, i, a, None, p0),
                              waits, ("dve", 1))
        return self.p.add("vector",
                          lambda e, o=out_ap, i=in_ap, a=s1, b=s2, p0=op0, p1=op1:
                          e.tensor_scalar(o, i, a, b, p0, p1),
                          waits, ("dve", 1))

    def tt(self, out_ap, a_ap, b_ap, op, waits=()):
        return self.p.add("vector",
                          lambda e, o=out_ap, a=a_ap, b=b_ap, p=op:
                          e.tensor_tensor(o, a, b, p), waits, ("dve", 1))

    def stt(self, out_ap, in0, scalar, in1, op0, op1, accum_out=None, waits=()):
        return self.p.add("vector",
                          lambda e, o=out_ap, a=in0, s=scalar, b=in1, p0=op0, p1=op1, ac=accum_out:
                          e.scalar_tensor_tensor(o, a, s, b, p0, p1, accum_out=ac),
                          waits, ("dve", 1))

    def copy(self, out_ap, in_ap, waits=()):
        return self.p.add("vector",
                          lambda e, o=out_ap, i=in_ap: e.tensor_copy(o, i),
                          waits, ("dve", 1))

    def reduce(self, out_ap, in_ap, op, absval=False, waits=()):
        return self.p.add("vector",
                          lambda e, o=out_ap, i=in_ap, p=op, a=absval:
                          e.tensor_reduce(o, i, AX, p, apply_absolute_value=a),
                          waits, ("dve", 1))

    def recip(self, out_ap, in_ap, waits=()):
        return self.p.add("vector",
                          lambda e, o=out_ap, i=in_ap: e.reciprocal(o, i),
                          waits, ("dve", 1))

    def memset(self, ap, val, waits=()):
        return self.p.add("vector",
                          lambda e, a=ap, v=val: e.memset(a, v),
                          waits, ("dve", 1))

    def barrier(self):
        toks = [(ss, self.p.counts[ss]) for ss in
                self.dma_sems + ["pe", "act", "dve"]]
        for eng in ("sync", "tensor", "scalar", "vector"):
            self.p.add(eng, None, waits=toks)

    def build(self):
        c, nc = self.c, self.nc
        S, D, INNER, DC = c.S, c.D, c.INNER, c.DC

        self.x_in = nc.dram_tensor("x", [c.SEQS, S, D], F32, kind="ExternalInput").ap()
        self.w_in = {}
        for n, shape in (("q", [D, D]), ("k", [D, D]), ("v", [D, D]), ("r", [D, D]),
                         ("g", [INNER, D]), ("u", [INNER, D]), ("d", [D, INNER])):
            self.w_in[n] = nc.dram_tensor("W" + n, shape, F32, kind="ExternalInput").ap()
        self.aux_in = nc.dram_tensor("aux", [128, 256], F32, kind="ExternalInput").ap()
        self.y_out = nc.dram_tensor("y", [c.SEQS, S, D], F32, kind="ExternalOutput").ap()

        self.wT_sc = {
            "g": self.dram("wgT_sc", [128, DC * INNER], BF16),
            "u": self.dram("wuT_sc", [128, DC * INNER], BF16),
            "d": self.dram("wdT_sc", [128, c.OC * D], BF16)}
        self.y1_sc = self.dram("y1_sc", [128, c.TC * D], F32)
        self.oth_sc = self.dram("oth_sc", [128, c.PAIRS * S], BF16)
        self.otl_sc = self.dram("otl_sc", [128, c.PAIRS * S], BF16)

        self.bank = [nc.alloc_psum_tensor(f"bank{i}", [128, 512], F32).ap()
                     for i in range(8)]
        self.bank_user = [None] * 8

        self.mask = self.sb("mask", [128, 128], F32)
        self.ident = self.sb("ident", [128, 128], F32)
        self.ones_bf = self.sb("ones_bf", [128, 128], BF16)
        self.ones_f = self.sb("ones_f", [128, 1], F32)
        self.ones_row = self.sb("ones_row", [1, 128], F32)
        self.negC = self.sb("negC", [128, 1], F32)
        self.negC50 = self.sb("negC50", [128, 1], F32)
        self.epsc = self.sb("epsc", [128, 1], F32)
        self.t_consts = [
            self.dma(self.mask, self.aux_in[:, 0:128]),
            self.dma(self.ident, self.aux_in[:, 128:256]),
            self.memset(self.ones_bf, 1.0),
            self.memset(self.ones_f, 1.0),
            self.memset(self.ones_row, 1.0),
            self.memset(self.negC, -C_STAB),
            self.memset(self.negC50, -C_STAB - 50.0),
            self.memset(self.epsc, 1e-5)]
        self.invsw = {}
        self.micols = {n: self.sb("micol_" + n, [128, 1], F32) for n in ("g", "u", "d")}
        self.wsplit_toks = {"q": [], "k": [], "v": []}
        self.wpre_tok = {}

        for s in range(c.SEQS):
            self.run_seq(s)
            self.barrier()
        self.p.add("sync", None,
                   waits=[(ss, self.p.counts[ss]) for ss in self.dma_sems])
        return self

    def pump_prep(self, n=1):
        g = getattr(self, "prep_gen", None)
        if g is None:
            return
        for _ in range(n):
            if next(g, "END") == "END":
                self.prep_gen = None
                return

    def prep_alloc(self, _es):
        c = self.c
        mx = max(c.D, min(c.INNER, 1024))
        al = lambda *a, **k: _es.enter_context(self.sbg(*a, **k)).ap()
        self.prep_bufs = dict(
            wbuf=[al(f"pw{i}", [128, mx], F32) for i in range(4)],
            acc=al("pacc", [128, 1], F32),
            acc2=al("pacc2", [128, 1], F32),
            tmp1=al("ptmp", [1, 4], F32),
            terf=al("pterf", [128, mx], F32),
            ter=al("pter", [128, mx], BF16),
            terT=al("pterT", [128, mx], BF16, is_transpose_dest=True),
            gcol=al("pgcol", [128, 1], F32))
        self.prep_state = {"t_use": [None] * 4, "t_gcol_free": None}

    def prep_weight_gen(self, wname):
        c = self.c
        D, INNER = c.D, c.INNER
        pb = self.prep_bufs
        wbuf, acc, acc2, tmp1 = pb["wbuf"], pb["acc"], pb["acc2"], pb["tmp1"]
        terf, ter, terT, gcol = pb["terf"], pb["ter"], pb["terT"], pb["gcol"]
        t_c = self.t_consts
        t_gcol_free = self.prep_state["t_gcol_free"]
        if True:
            for wname, rows, cols in ((wname, {"g": INNER, "u": INNER,
                                               "d": D}[wname],
                                       {"g": D, "u": D, "d": INNER}[wname]),):
                w_in, sc = self.w_in[wname], self.wT_sc[wname]
                nch = rows // 128
                t_use = self.prep_state["t_use"]
                t_acc = None
                ceff1 = min(cols, 1024)
                n1 = nch * (cols // ceff1)
                t_lds = {}

                def issue1(rr):
                    r, cf = divmod(rr, cols // ceff1)
                    b = rr % 4
                    t_lds[rr] = self.dma(wbuf[b][:, 0:ceff1],
                                         w_in[r * 128:(r + 1) * 128,
                                              cf * ceff1:(cf + 1) * ceff1],
                                         waits=[t_use[b]])
                for rr in range(min(3, n1)):
                    issue1(rr)
                for rr in range(n1):
                    if rr + 3 < n1:
                        issue1(rr + 3)
                    b = rr % 4
                    if rr == 0:
                        t_acc = self.reduce(acc, wbuf[b][:, 0:ceff1], ALU.add,
                                            absval=True, waits=[t_lds[rr]])
                    else:
                        t_r = self.reduce(acc2, wbuf[b][:, 0:ceff1], ALU.add,
                                          absval=True,
                                          waits=[t_lds[rr], t_acc])
                        t_acc = self.tt(acc, acc, acc2, ALU.add, waits=[t_r])
                    t_use[b] = t_acc
                    yield
                t_mm = self.mm(self.bank[0][0:1, 0:1], self.ones_f, acc, True,
                               True, waits=[t_acc, self.bank_user[0]] + t_c)
                t_mean = self.ts(tmp1[0:1, 0:1], self.bank[0][0:1, 0:1],
                                 1.0 / float(rows * cols), None, ALU.mult,
                                 waits=[t_mm])
                self.bank_user[0] = t_mean
                t_cl = self.ts(tmp1[0:1, 1:2], tmp1[0:1, 0:1], 1e-5, None,
                               ALU.max, waits=[t_mean])
                t_sw = self.recip(tmp1[0:1, 2:3], tmp1[0:1, 1:2], waits=[t_cl])
                micol = self.micols[wname]
                t_b1 = self.mm(self.bank[0][:, 0:1], self.ones_row,
                               tmp1[0:1, 2:3], True, True,
                               waits=[t_sw, self.bank_user[0]])
                t_g = self.copy(gcol, self.bank[0][:, 0:1],
                                waits=[t_b1, t_gcol_free])
                self.bank_user[0] = t_g
                t_b2 = self.mm(self.bank[0][:, 0:1], self.ones_row,
                               tmp1[0:1, 1:2], True, True,
                               waits=[t_cl, self.bank_user[0]])
                t_mi = self.copy(micol, self.bank[0][:, 0:1], waits=[t_b2])
                self.bank_user[0] = t_mi
                self.invsw[wname] = (micol, t_mi)
                yield

                t_terf_free = None
                t_ter_free = None
                t_terT_free = None
                ceff = min(cols, 1024)
                ncf = cols // ceff
                nchc = ceff // 128
                n2 = nch * ncf
                t_lds2 = {}

                def issue2(idx):
                    r, cf = divmod(idx, ncf)
                    b = idx % 4
                    t_lds2[idx] = self.dma(
                        wbuf[b][:, 0:ceff],
                        w_in[r * 128:(r + 1) * 128,
                             cf * ceff:(cf + 1) * ceff],
                        waits=[t_use[b]])
                for idx in range(min(3, n2)):
                    issue2(idx)
                for idx in range(n2):
                    r, cf = divmod(idx, ncf)
                    if True:
                        if idx + 3 < n2:
                            issue2(idx + 3)
                        b = idx % 4
                        t1 = self.ts(terf[:, 0:ceff], wbuf[b][:, 0:ceff],
                                     gcol, MAGIC, ALU.mult, ALU.add,
                                     waits=[t_lds2[idx], t_g, t_terf_free])
                        yield
                        t2 = self.ts(terf[:, 0:ceff], terf[:, 0:ceff], -MAGIC,
                                     1.0, ALU.add, ALU.min, waits=[t1])
                        yield
                        t3 = self.ts(ter[:, 0:ceff], terf[:, 0:ceff], -1.0,
                                     None, ALU.max, waits=[t2, t_ter_free])
                        t_use[b] = t3
                        t_terf_free = t3
                        t_gcol_free = t1
                        t_tr = self.dmat(
                            terT[:, 0:ceff].rearrange("p (ch q) -> p ch q",
                                                      ch=nchc),
                            ter[:, 0:ceff], waits=[t3, t_terT_free])
                        t_ter_free = t_tr
                        t_st = self.dma(
                            sc.rearrange("p (ch rr) -> p ch rr",
                                         ch=cols // 128)[
                                :, cf * nchc:(cf + 1) * nchc,
                                r * 128:(r + 1) * 128],
                            terT[:, 0:ceff].rearrange("p (ch q) -> p ch q",
                                                      ch=nchc),
                            waits=[t_tr])
                        t_terT_free = t_st
                        yield
            self.prep_state["t_gcol_free"] = t_gcol_free

    def run_seq(self, s):
        c = self.c
        S, D, DC = c.S, c.D, c.DC
        with ExitStack() as _es0:
            XT_hi = _es0.enter_context(self.sbg("XT_hi", [128, DC * S], BF16, is_transpose_dest=True)).ap()
            XT_lo = _es0.enter_context(self.sbg("XT_lo", [128, DC * S], BF16, is_transpose_dest=True)).ap()
            with ExitStack() as _es1:
                x_tm = _es1.enter_context(self.sbg("x_tm", [128, c.TC * D], F32, side="right")).ap()
                xhi = _es1.enter_context(self.sbg("xhi", [128, c.TC * D], BF16)).ap()
                xlo = _es1.enter_context(self.sbg("xlo", [128, c.TC * D], BF16)).ap()
                nx = max(1, c.TC // 4)
                t_x4 = [self.dma(
                    x_tm[:, i * 4 * D:(i + 1) * 4 * D]
                    .rearrange("p (tc d) -> p tc d", tc=4),
                    self.x_in[s][i * 512:(i + 1) * 512, :]
                    .rearrange("(tc p) d -> p tc d", p=128))
                    for i in range(nx)]
                t_hi = self.copy(xhi, x_tm, waits=t_x4)
                t_lo = self.stt(xlo, x_tm, 1.0, xhi, ALU.mult, ALU.subtract,
                                waits=[t_hi])
                t_Xh_l, t_Xl_l = [], []
                for g in range(max(1, c.TC // 4)):
                    cs = slice(g * 4 * D, min((g + 1) * 4, c.TC) * D)
                    t_Xh_l.append(self.dmat(
                        XT_hi[:, cs].rearrange("p (blk tl) -> p blk tl",
                                               tl=128),
                        xhi[:, cs], waits=[t_hi]))
                    t_Xl_l.append(self.dmat(
                        XT_lo[:, cs].rearrange("p (blk tl) -> p blk tl",
                                               tl=128),
                        xlo[:, cs], waits=[t_lo]))
            self.t_XT_parts = (t_Xh_l, t_Xl_l)

            wstage = _es0.enter_context(
                self.sbg("wstage", [128, 3 * D], F32)).ap()
            _al0 = lambda *a, **k: _es0.enter_context(self.sbg(*a, **k)).ap()
            nbufs = (_al0("rls", [128, 512], F32),
                     _al0("rlb", [128, 1024], BF16),
                     _al0("osr", [64, 512], F32),
                     _al0("onf", [64, 512], F32),
                     [_al0("ogh0", [64, 512], BF16),
                      _al0("ogh1", [64, 512], BF16)],
                     [_al0("ogl0", [64, 512], BF16),
                      _al0("ogl1", [64, 512], BF16)])
            self.prefetch_w(0, wstage)
            if s == 0:
                with ExitStack() as _esp:
                    self.prep_alloc(_esp)
                    def _gen():
                        for w in ("g", "u", "d"):
                            yield from self.prep_weight_gen(w)
                    self.prep_gen = _gen()
                    for pair in range(c.PAIRS):
                        self.attn_pair(s, pair, XT_hi, XT_lo, wstage, nbufs)
                    self.flush_norm()
                    self.pump_prep(10 ** 6)
            else:
                for pair in range(c.PAIRS):
                    self.attn_pair(s, pair, XT_hi, XT_lo, wstage, nbufs)
                self.flush_norm()
        self.barrier()

        with ExitStack() as _es2:
            XqT = _es2.enter_context(self.sbg("XqT", [128, DC * S], BF16, is_transpose_dest=True)).ap()
            cg_cols = _es2.enter_context(self.sbg("cg_cols", [128, c.TC], F32)).ap()
            cu_cols = _es2.enter_context(self.sbg("cu_cols", [128, c.TC], F32)).ap()
            self.wr_ln1(s, XqT, cg_cols, cu_cols)
            self.barrier()
            self.ffn(s, XqT, cg_cols, cu_cols)

    def flush_norm(self):
        if self.norm_pending is not None:
            self.norm_pending[0]()
            self.norm_pending[1]()
            self.norm_pending = None

    def t_XT_for(self, qp):
        h, l = self.t_XT_parts
        return [h[-1], l[-1]]

    def prefetch_w(self, pair, wstage):
        c = self.c
        for i, name in enumerate(("q", "k", "v")):
            src = self.w_in[name][:, pair * 128:(pair + 1) * 128] \
                .rearrange("(dh dl) j -> dl dh j", dl=128)
            self.wpre_tok[name] = self.dma(
                wstage.rearrange("p (n dh j) -> p n dh j", n=3, dh=c.DC)[:, i],
                src, waits=self.wsplit_toks[name])

    def attn_pair(self, s, pair, XT_hi, XT_lo, wstage, nbufs):
        c = self.c
        S, D, DC, QT, KC = c.S, c.D, c.DC, c.QT, c.KC
        with ExitStack() as _es:
            al = lambda *a, **k: _es.enter_context(self.sbg(*a, **k)).ap()

            wh = al("apwh", [128, D], BF16)
            wl = al("apwl", [128, D], BF16)
            qk = {"q": (al("qhi", [128, S], BF16), al("qlo", [128, S], BF16))}
            khp = [al("khp0", [128, S], BF16), al("khp1", [128, S], BF16)]
            khl = [al("khl0", [128, S], BF16), al("khl1", [128, S], BF16)]
            klp = [al("klp0", [128, S], BF16), al("klp1", [128, S], BF16)]

            vp = [al("vp0", [128, KC * 65], F32), al("vp1", [128, KC * 65], F32)]
            abuf = al("abuf", [128, KC * 512], F32)
            vts = abuf[:, 0:S]
            rls, rlb, osr, onf, ogh, ogl = nbufs

            def load_split(name):
                slot = {"q": 0, "k": 1, "v": 2}[name]
                src = wstage[:, slot * D:(slot + 1) * D]
                t_h = self.copy(wh, src, waits=[self.wpre_tok[name]]
                                + list(self.t_wsplit_done))
                t_l = self.stt(wl, src, 1.0, wh, ALU.mult, ALU.subtract,
                               waits=[t_h])
                self.wsplit_toks[name] = [t_h, t_l]
                return [t_h, t_l]

            XhR = XT_hi.rearrange("p (tcc dh tl) -> p tcc dh tl",
                                  tcc=S // 128, dh=DC)
            XlR = XT_lo.rearrange("p (tcc dh tl) -> p tcc dh tl",
                                  tcc=S // 128, dh=DC)

            def project(t_w, evac):
                toks = []
                grp = [None, None]
                for qp in range((QT + 1) // 2):
                    js = [j for j in (0, 1) if 2 * qp + j < QT]
                    grp = [None, None]
                    for ph, (wop, xop) in enumerate(((wh, XhR), (wh, XlR),
                                                     (wl, XhR))):
                        for dh in range(DC):
                            last = (ph == 2 and dh == DC - 1)
                            for j in js:
                                qt = 2 * qp + j
                                bk = (2 * (qp % 2) + j) if QT > 2 else j
                                w0 = (t_w + [self.bank_user[bk]]
                                      + self.t_XT_for(qp)) \
                                    if (ph == 0 and dh == 0) else ()
                                grp[j] = self.mm(
                                    self.bank[bk],
                                    wop[:, dh * 128:(dh + 1) * 128],
                                    xop[:, 4 * qt:4 * qt + 4, dh],
                                    ph == 0 and dh == 0, last, waits=w0)
                    for j in js:
                        bk = (2 * (qp % 2) + j) if QT > 2 else j
                        toks.append(evac(2 * qp + j, bk, grp[j]))
                return toks, [g for g in grp if g is not None]

            t_w = load_split("q")
            hi, lo = qk["q"]

            def evac_q(qt, bk, t_grp, hi=hi, lo=lo):
                th = self.copy(hi[:, qt * 512:(qt + 1) * 512], self.bank[bk],
                               waits=[t_grp])
                tl = self.stt(lo[:, qt * 512:(qt + 1) * 512], self.bank[bk],
                              1.0, hi[:, qt * 512:(qt + 1) * 512],
                              ALU.mult, ALU.subtract, waits=[th])
                self.bank_user[bk] = tl
                return tl
            t_qk, q_grps = project(t_w, evac_q)
            self.t_wsplit_done = q_grps

            t_w = load_split("k")
            t_kz = [self.memset(t, 0.0, waits=[self.last_score_tok])
                    for t in khp]

            def evac_k(qt, bk, t_grp):
                toks = []
                for h in (0, 1):
                    rs = slice(64 * h, 64 * h + 64)
                    cs = slice(qt * 512, (qt + 1) * 512)
                    th = self.copy(khp[h][rs, cs], self.bank[bk][rs, :],
                                   waits=[t_grp] + t_kz)
                    th2 = self.copy(khl[h][rs, cs], self.bank[bk][rs, :],
                                    waits=[t_grp, self.last_score_tok])
                    tl = self.stt(klp[h][rs, cs], self.bank[bk][rs, :], 1.0,
                                  khp[h][rs, cs], ALU.mult, ALU.subtract,
                                  waits=[th] + (self.t_qrep_done or []))
                    toks += [th2, tl]
                self.bank_user[bk] = toks[-1]
                return toks[-1]
            t_qk, k_grps = project(t_w, evac_k)
            self.t_wsplit_done = k_grps

            t_w = load_split("v")

            def evac_v(qt, bk, t_grp):
                tv = self.copy(vts[:, qt * 512:(qt + 1) * 512], self.bank[bk],
                               waits=[t_grp, self.ab_user.get(qt)])
                self.bank_user[bk] = tv
                return tv
            t_v, v_grps = project(t_w, evac_v)
            self.t_wsplit_done = v_grps
            if pair + 1 < c.PAIRS:
                self.prefetch_w(pair + 1, wstage)

            t_qrep = []
            for h in (0, 1):
                ro = slice(64 * (1 - h), 64 * (1 - h) + 64)
                rh = slice(64 * h, 64 * h + 64)
                t_sh = self.dma(khl[h][ro, :], klp[h][rh, :],
                                waits=[t_qk[-1], self.last_score_tok])
                t_q1 = self.copy(klp[h][rh, :], qk["q"][0][rh, :],
                                 waits=[t_sh, t_qk[-1]])
                t_q2 = self.dma(klp[h][ro, :], qk["q"][0][rh, :],
                                waits=[t_sh, t_qk[-1]])
                t_qrep += [t_sh, t_q1, t_q2]
            self.t_qrep_done = t_qrep
            qrep = klp

            t_ones = [self.memset(
                vp[h].rearrange("p (kc cc) -> p kc cc", cc=65)[:, :, 64:65],
                1.0, waits=[self.last_av_tok]) for h in (0, 1)]
            t_vp_last = None
            for kc in range(KC):
                bk_t = (6, 0)[kc % 2]
                t_t = self.tr(self.bank[bk_t][:, 0:128],
                              vts[:, kc * 128:(kc + 1) * 128],
                              waits=[t_v[(kc * 128) // 512],
                                     self.bank_user[bk_t], self.t_consts[1]])
                tc0 = self.copy(vp[0][:, kc * 65: kc * 65 + 64],
                                self.bank[bk_t][:, 0:64], waits=[t_t] + t_ones)
                tc1 = self.copy(vp[1][:, kc * 65: kc * 65 + 64],
                                self.bank[bk_t][:, 64:128], waits=[t_t])
                self.bank_user[bk_t] = tc1
                t_vp_last = tc1

            self.avslot = 1
            t_ot_last = None
            for h in (0, 1):
                hi_q, lo_q = qk["q"]
                for qt in range(QT):
                    avb = 2 + (self.avslot % 3)
                    self.avslot += 1
                    t_av_grp = None
                    kc_hi = min(KC, 4 * qt + 4)
                    if self.norm_pending is not None:
                        self.norm_pending[0]()
                        self.pump_prep(3)
                    t_exp = [None] * kc_hi
                    for kc in range(kc_hi):
                        bk = (0, 1, 6, 7)[kc % 4]
                        w0 = [self.bank_user[bk], t_qk[-1], t_v[-1]] + \
                            self.t_qrep_done
                        t_s = self.mm(self.bank[bk],
                                      khl[h][:, kc * 128:(kc + 1) * 128],
                                      qrep[h][:, qt * 512:(qt + 1) * 512],
                                      True, False, waits=w0)
                        t_s = self.mm(self.bank[bk],
                                      khp[h][:, kc * 128:(kc + 1) * 128],
                                      lo_q[:, qt * 512:(qt + 1) * 512],
                                      False, True, waits=())
                        self.last_score_tok = t_s
                        a_off = kc * 128 - qt * 512
                        ab = abuf[:, kc * 512: kc * 512 + 512]
                        w_e = [t_s, self.ab_user.get(kc), t_vp_last]
                        t_el = []
                        if a_off + 128 <= 0:
                            t_el.append(self.act(ab, self.bank[bk], AF.Exp,
                                                 bias=self.negC, scale=0.125,
                                                 waits=w_e))
                        else:
                            t_m = self.tt(self.bank[bk][:, a_off:a_off + 128],
                                          self.bank[bk][:, a_off:a_off + 128],
                                          self.mask, ALU.add,
                                          waits=[t_s, self.t_consts[0]])
                            w_e = [t_m, self.ab_user.get(kc), t_vp_last]
                            if a_off > 0:
                                t_el.append(self.act(ab[:, 0:a_off],
                                                     self.bank[bk][:, 0:a_off],
                                                     AF.Exp, bias=self.negC50,
                                                     scale=0.125, waits=w_e))
                            t_el.append(self.act(ab[:, a_off:512],
                                                 self.bank[bk][:, a_off:512],
                                                 AF.Exp, bias=self.negC,
                                                 scale=0.125, waits=w_e))
                        self.bank_user[bk] = t_el[-1]
                        t_exp[kc] = t_el
                    if self.norm_pending is not None:
                        self.norm_pending[1]()
                        self.norm_pending = None
                    for kc in range(kc_hi):
                        w_av = list(t_exp[kc]) + [t_vp_last]
                        if kc == 0:
                            w_av.append(self.bank_user[avb])
                        t_av_grp = self.mm(self.bank[avb][0:65, :],
                                           vp[h][:, kc * 65:(kc + 1) * 65],
                                           abuf[:, kc * 512: kc * 512 + 512],
                                           kc == 0, kc == kc_hi - 1,
                                           waits=w_av, force_inc=True)
                        self.ab_user[kc] = t_av_grp
                    self.last_av_tok = t_av_grp

                    st_rl = [None, None]

                    def norm_pre(avb=avb, t_av_grp=t_av_grp, st_rl=st_rl):
                        st_rl[0], st_rl[1] = self.do_norm_pre(
                            avb, t_av_grp, rls, rlb)

                    def norm_rest(avb=avb, h=h, qt=qt, t_av_grp=t_av_grp,
                                  st_rl=st_rl):
                        self.do_normalize(pair, avb, h, qt, t_av_grp,
                                          st_rl[0], st_rl[1], rls, rlb,
                                          osr, onf, ogh, ogl)
                    self.norm_pending = (norm_pre, norm_rest)
            t_ot_last = self.t_ot_store
        self.t_ot_store = t_ot_last

    def do_norm_pre(self, avb, t_av_grp, rls, rlb):
        t_rl = self.recip(rls[64:65, :], self.bank[avb][64:65, :],
                          waits=[t_av_grp, self.nrm_prev])
        t_rh = self.copy(rlb[64:65, 0:512], rls[64:65, :],
                         waits=[t_rl])
        t_rw = self.stt(rlb[64:65, 512:1024], rls[64:65, :], 1.0,
                        rlb[64:65, 0:512], ALU.mult, ALU.subtract,
                        waits=[t_rh])
        return t_rh, t_rw

    def do_normalize(self, pair, avb, h, qt, t_av_grp, t_rh, t_rw, rls, rlb,
                     osr, onf, ogh, ogl):
        c = self.c
        S = c.S
        if True:
            if True:
                if True:
                    self.mm(self.bank[5][0:64, :], self.ones_bf[64:65, 0:64],
                            rlb[64:65, 0:512], True, False,
                            waits=[t_rh, t_rw, self.bank_user[5],
                                   self.t_consts[2]], inc=False)
                    t_b2 = self.mm(self.bank[5][0:64, :],
                                   self.ones_bf[64:65, 0:64],
                                   rlb[64:65, 512:1024], False, True, waits=())
                    t_or = self.copy(osr, self.bank[avb][0:64, :],
                                     waits=[t_av_grp, self.nrm_prev])
                    self.bank_user[avb] = t_or
                    t_on = self.tt(onf, osr, self.bank[5][0:64, :], ALU.mult,
                                   waits=[t_or, t_b2])
                    self.bank_user[5] = t_on
                    par = self.avslot % 2
                    t_oh = self.copy(ogh[par], onf,
                                     waits=[t_on] + (self.og_store[par] or []))
                    t_ol = self.stt(ogl[par], onf, 1.0, ogh[par], ALU.mult,
                                    ALU.subtract, waits=[t_oh])
                    rs = slice(64 * h, 64 * h + 64)
                    cs = slice(pair * S + qt * 512, pair * S + (qt + 1) * 512)
                    t_s1 = self.dma(self.oth_sc[rs, cs], ogh[par],
                                    waits=[t_ol])
                    t_s2 = self.dma(self.otl_sc[rs, cs], ogl[par],
                                    waits=[t_ol])
                    self.og_store[par] = [t_s1, t_s2]
                    self.nrm_prev = t_ol
                    self.t_ot_store = t_s2
                    self.pump_prep(4)

    def wr_ln1(self, s, XqT, cg_cols, cu_cols):
        c = self.c
        S, D, DC, JT = c.S, c.D, c.DC, c.JT
        with ExitStack() as _es:
            al = lambda *a, **k: _es.enter_context(self.sbg(*a, **k)).ap()
            wrf = al("wrf", [128, DC * D], F32)
            wrh = al("wrh", [128, DC * D], BF16)
            wrl = al("wrl", [128, DC * D], BF16)
            othb = al("oth_b", [128, 4 * DC * 128], BF16)
            otlb = al("otl_b", [128, 4 * DC * 128], BF16)
            xrd = al("xrd", [128, 4 * D], F32)
            zbuf = al("zbuf", [128, D], F32)
            zsq = al("zsq", [128, D], F32)
            y1t = al("y1t", [128, D], F32)
            ubuf = al("ubuf", [128, D], F32)
            xqtm = al("xqtm", [128, c.TC * D], BF16)
            st = al("stat", [128, 12], F32)

            t_wr = self.dma(wrf.rearrange("p (dvc j) -> p dvc j", dvc=DC),
                            self.w_in["r"].rearrange("(dvc dl) j -> dl dvc j",
                                                     dl=128),
                            waits=list(self.t_wsplit_done))
            t_wh = self.copy(wrh, wrf, waits=[t_wr])
            t_wl = self.stt(wrl, wrf, 1.0, wrh, ALU.mult, ALU.subtract,
                            waits=[t_wh])

            t_slab_use = [None] * 4
            t_x_use = [None] * 4
            prev_apply = None
            prev_u = None
            prev_y1_store = None
            prev_ms = None
            t_xq_list = []
            lds = {}

            def load_tc(tc):
                b = tc % 4
                osl = slice(b * DC * 128, (b + 1) * DC * 128)
                src_h = self.oth_sc.rearrange(
                    "p (pr t) -> p pr t", pr=c.PAIRS)[
                    :, :, tc * 128:(tc + 1) * 128]
                src_l = self.otl_sc.rearrange(
                    "p (pr t) -> p pr t", pr=c.PAIRS)[
                    :, :, tc * 128:(tc + 1) * 128]
                w_sl = [t_slab_use[b], self.t_ot_store]
                t_s1 = self.dma(othb[:, osl].rearrange(
                    "p (pr t) -> p pr t", pr=DC), src_h, waits=w_sl)
                t_s2 = self.dma(otlb[:, osl].rearrange(
                    "p (pr t) -> p pr t", pr=DC), src_l, waits=w_sl)
                t_xl = self.dma(xrd[:, b * D:(b + 1) * D],
                                self.x_in[s, tc * 128:(tc + 1) * 128, :],
                                waits=[t_x_use[b]])
                lds[tc] = (t_s1, t_s2, t_xl)
            for tc in range(min(3, c.TC)):
                load_tc(tc)
            for tc in range(c.TC):
                if tc + 3 < c.TC:
                    load_tc(tc + 3)
                b = tc % 4
                t_s1, t_s2, t_xl = lds[tc]
                t_zs = []
                t_grp_last = None
                grp = [None] * JT
                for ph, (oop, wop) in enumerate(((othb, wrh), (othb, wrl),
                                                 (otlb, wrh))):
                    for dvc in range(DC):
                        last = (ph == 2 and dvc == DC - 1)
                        for jt in range(JT):
                            bk = 4 + 2 * (tc % 2) + jt
                            w0 = ([t_s1, t_s2, t_wl, self.bank_user[bk]]
                                  if (ph == 0 and dvc == 0) else ())
                            grp[jt] = self.mm(
                                self.bank[bk],
                                oop[:, b * DC * 128 + dvc * 128:
                                    b * DC * 128 + dvc * 128 + 128],
                                wop[:, dvc * D + jt * 512:
                                    dvc * D + jt * 512 + 512],
                                ph == 0 and dvc == 0, last, waits=w0)
                for jt in range(JT):
                    bk = 4 + 2 * (tc % 2) + jt
                    t_grp = grp[jt]
                    t_grp_last = t_grp
                    t_z = self.stt(zbuf[:, jt * 512:(jt + 1) * 512],
                                   self.bank[bk], 1.0,
                                   xrd[:, b * D + jt * 512:
                                       b * D + jt * 512 + 512],
                                   ALU.mult, ALU.add,
                                   accum_out=st[:, jt:jt + 1],
                                   waits=[t_grp, t_xl, prev_apply])
                    self.bank_user[bk] = t_z
                    t_zs.append(t_z)
                t_slab_use[b] = t_grp_last
                t_x_use[b] = t_zs[-1]
                if JT > 1:
                    t_sum = self.tt(st[:, 2:3], st[:, 0:1], st[:, 1:2], ALU.add,
                                    waits=t_zs)
                else:
                    t_sum = self.copy(st[:, 2:3], st[:, 0:1], waits=t_zs)
                t_sq = self.act(zsq, zbuf, AF.Square, accum_out=st[:, 3:4],
                                waits=t_zs + [prev_ms])
                t_mean = self.ts(st[:, 4:5], st[:, 2:3], 1.0 / D, None,
                                 ALU.mult, waits=[t_sum])
                t_ms = self.ts(st[:, 5:6], st[:, 3:4], 1.0 / D, None, ALU.mult,
                               waits=[t_sq])
                prev_ms = t_ms
                t_m2 = self.ts(st[:, 6:7], st[:, 4:5], st[:, 4:5], None,
                               ALU.mult, waits=[t_mean])
                t_var = self.tt(st[:, 7:8], st[:, 5:6], st[:, 6:7],
                                ALU.subtract, waits=[t_ms, t_m2])
                t_sd = self.act(st[:, 8:9], st[:, 7:8], AF.Sqrt, bias=self.epsc,
                                waits=[t_var, self.t_consts[-1]])
                t_rs = self.recip(st[:, 9:10], st[:, 8:9],
                                  waits=[t_sd, prev_apply])
                t_lb = self.stt(st[:, 10:11], st[:, 4:5], -1.0, st[:, 9:10],
                                ALU.mult, ALU.mult, waits=[t_rs, prev_apply])
                t_y1 = self.ts(y1t, zbuf, st[:, 9:10], st[:, 10:11],
                               ALU.mult, ALU.add,
                               waits=[t_lb, prev_u, prev_y1_store])
                prev_apply = t_y1
                t_st1 = self.dma(self.y1_sc[:, tc * D:(tc + 1) * D], y1t,
                                 waits=[t_y1])
                prev_y1_store = t_st1
                t_am = self.reduce(st[:, 11:12], y1t, ALU.max, absval=True,
                                   waits=[t_y1])
                t_ac = self.ts(st[:, 0:1], st[:, 11:12], 1e-5, None, ALU.max,
                               waits=[t_am])
                t_rc = self.recip(st[:, 1:2], st[:, 0:1], waits=[t_ac])
                t_sx = self.ts(st[:, 2:3], st[:, 1:2], 127.0, None, ALU.mult,
                               waits=[t_rc])
                micol, t_mi = self.invsw["g"]
                t_cg = self.stt(cg_cols[:, tc:tc + 1], st[:, 0:1], 1.0 / 127.0,
                                micol, ALU.mult, ALU.mult, waits=[t_ac, t_mi])
                micol_u, t_miu = self.invsw["u"]
                t_cu = self.stt(cu_cols[:, tc:tc + 1], st[:, 0:1], 1.0 / 127.0,
                                micol_u, ALU.mult, ALU.mult,
                                waits=[t_ac, t_miu])

                t_u = self.ts(ubuf, y1t, st[:, 2:3], MAGIC, ALU.mult, ALU.add,
                              waits=[t_sx])
                prev_u = t_u
                t_xq = self.ts(xqtm[:, tc * D:(tc + 1) * D], ubuf, -MAGIC, None,
                               ALU.add, waits=[t_u])
                t_xq_list.append(t_xq)
                self.t_cg_done = t_cg
                self.t_cu_done = t_cu
            self.t_xq_parts = []
            for g in range(c.TC // c.TSB):
                cs = slice(g * c.TSB * D, (g + 1) * c.TSB * D)
                self.t_xq_parts.append(self.dmat(
                    XqT[:, cs].rearrange("p (blk tl) -> p blk tl", tl=128),
                    xqtm[:, cs],
                    waits=[t_xq_list[tc]
                           for tc in range(g * c.TSB, (g + 1) * c.TSB)]))
            self.t_xq_done = self.t_xq_parts[-1]
            if self.c.debug:
                dbg1 = self.dram("dbg_XqT", [128, DC * S], BF16)
                dbg2 = self.dram("dbg_cg", [128, c.TC], F32)
                self.dma(dbg1, XqT, waits=[self.t_xq_done])
                self.dma(dbg2, cg_cols, waits=[self.t_cg_done])

    def ffn(self, s, XqT, cg_cols, cu_cols):
        c = self.c
        S, D, DC, JT, INNER = c.S, c.D, c.DC, c.JT, c.INNER
        OC, OTILE, TSB = c.OC, c.OTILE, c.TSB
        HB = TSB // 2 or 1
        NH = TSB // HB
        WOC = 2
        with ExitStack() as _es:
            al = lambda *a, **k: _es.enter_context(self.sbg(*a, **k)).ap()
            htsb = al("htsb", [128, TSB * INNER], F32)
            hq = al("hq", [128, 2 * INNER], BF16)
            hqT = al("hqT", [128, TSB * OC * 128], BF16, is_transpose_dest=True)
            wgsb = al("wgsb", [128, 2 * DC * 512], BF16)
            wusb = al("wusb", [128, 2 * DC * 512], BF16)
            wdsb = al("wdsb", [128, 2 * WOC * D], BF16)
            gts = al("gts", [128, 2 * 512], F32)
            z2 = [al("z2a", [128, D], F32), al("z2b", [128, D], F32)]
            sqb = al("sqb", [128, D], F32)
            cdc = al("cdc", [128, 2 * TSB], F32)
            pmax = al("pmax", [128, TSB * OTILE], F32)
            st = al("stat2", [128, 28], F32)
            micol_d, t_mid = self.invsw["d"]

            state = dict(
                t_g_use=[None, None], t_u_use=[None, None],
                t_d_use=[None, None], ep_i=0,
                gts_user=[None, None],
                t_hq_free=[None, None],
                t_hqT_use=[None] * TSB,
                t_h_free=[None] * TSB,
                t_cd_use=[None] * (2 * TSB),
                z2_users=[[], []],
                pm_use=[None] * TSB,
                prev_qtok=None)

            def load_a(ot):
                b = ot % 2
                wg_s = wgsb[:, b * DC * 512:(b + 1) * DC * 512]
                wu_s = wusb[:, b * DC * 512:(b + 1) * DC * 512]
                t_lg = self.dma(
                    wg_s.rearrange("p (dh o) -> p dh o", dh=DC),
                    self.wT_sc["g"].rearrange("p (dh o) -> p dh o", dh=DC)[
                        :, :, ot * 512:(ot + 1) * 512],
                    waits=[state["t_g_use"][b]])
                t_lu = self.dma(
                    wu_s.rearrange("p (dh o) -> p dh o", dh=DC),
                    self.wT_sc["u"].rearrange("p (dh o) -> p dh o", dh=DC)[
                        :, :, ot * 512:(ot + 1) * 512],
                    waits=[state["t_u_use"][b]])
                return t_lg, t_lu

            def phase_a(tsb, pre=None):
                t_h_parts = [[] for _ in range(TSB)]
                t_pm_parts = [[] for _ in range(TSB)]
                for ot in range(OTILE):
                    b = ot % 2
                    wg_s = wgsb[:, b * DC * 512:(b + 1) * DC * 512]
                    wu_s = wusb[:, b * DC * 512:(b + 1) * DC * 512]
                    if pre is not None and ot < len(pre):
                        t_lg, t_lu = pre[ot]
                    else:
                        t_lg, t_lu = load_a(ot)
                    for ti in range(TSB):
                        tc = tsb * TSB + ti
                        hsl = htsb[:, ti * INNER:(ti + 1) * INNER]
                        bg, bu = 2 * b, 2 * b + 1
                        w0g = [t_lg, self.bank_user[bg],
                               self.t_xq_parts[tsb]]
                        w0u = [t_lu, self.bank_user[bu],
                               self.t_xq_parts[tsb]]
                        tg_grp = tu_grp = None
                        for dh in range(DC):
                            tg_grp = self.mm(
                                self.bank[bg],
                                XqT[:, tc * D + dh * 128:
                                    tc * D + dh * 128 + 128],
                                wg_s[:, dh * 512:(dh + 1) * 512],
                                dh == 0, dh == DC - 1,
                                waits=w0g if dh == 0 else ())
                        for dh in range(DC):
                            tu_grp = self.mm(
                                self.bank[bu],
                                XqT[:, tc * D + dh * 128:
                                    tc * D + dh * 128 + 128],
                                wu_s[:, dh * 512:(dh + 1) * 512],
                                dh == 0, dh == DC - 1,
                                waits=w0u if dh == 0 else ())
                        gsl = gts[:, b * 512:(b + 1) * 512]
                        t_si = self.act(gsl, self.bank[bg], AF.Silu,
                                        scale=cg_cols[:, tc:tc + 1],
                                        waits=[tg_grp, self.t_cg_done,
                                               state["gts_user"][b]])
                        self.bank_user[bg] = t_si
                        t_hc = self.tt(hsl[:, ot * 512:(ot + 1) * 512],
                                       self.bank[bu], gsl, ALU.mult,
                                       waits=[t_si, tu_grp,
                                              state["t_h_free"][ti]])
                        self.bank_user[bu] = t_hc
                        state["gts_user"][b] = t_hc
                        t_h_parts[ti].append(t_hc)
                        t_pm_parts[ti].append(self.reduce(
                            pmax[:, ti * OTILE + ot:ti * OTILE + ot + 1],
                            hsl[:, ot * 512:(ot + 1) * 512], ALU.max,
                            absval=True,
                            waits=[t_hc, state["pm_use"][ti]]))
                    state["t_g_use"][b] = t_h_parts[TSB - 1][-1]
                    state["t_u_use"][b] = t_h_parts[TSB - 1][-1]
                return t_h_parts, t_pm_parts

            def quant(tsb, hparts, tis, tr_out=None):
                t_h_parts, t_pm_parts = hparts
                off = (tsb % 2) * TSB
                for ti in tis:
                    tc = tsb * TSB + ti
                    hs = ti % 2
                    hsl = htsb[:, ti * INNER:(ti + 1) * INNER]
                    t_am = self.reduce(
                        st[:, 0:1], pmax[:, ti * OTILE:(ti + 1) * OTILE],
                        ALU.max, absval=True,
                        waits=t_pm_parts[ti] + [state["prev_qtok"]])
                    state["pm_use"][ti] = t_am
                    t_ac = self.ts(st[:, 1:2], st[:, 0:1], 1e-5, None, ALU.max,
                                   waits=[t_am])
                    t_rc = self.recip(st[:, 2:3], st[:, 1:2], waits=[t_ac])
                    t_sh = self.ts(st[:, 3:4], st[:, 2:3], 127.0, None,
                                   ALU.mult, waits=[t_rc])
                    t_cd0 = self.stt(st[:, 15:16], st[:, 1:2], 1.0 / 127.0,
                                     cu_cols[:, tc:tc + 1], ALU.mult, ALU.mult,
                                     waits=[t_ac, self.t_cu_done,
                                            state["t_cd_use"][off + ti]])
                    t_cd = self.tt(cdc[:, off + ti:off + ti + 1],
                                   st[:, 15:16], micol_d,
                                   ALU.mult, waits=[t_cd0, t_mid])
                    state["t_cd_use"][off + ti] = t_cd
                    t_u2 = self.ts(hsl, hsl, st[:, 3:4], MAGIC, ALU.mult,
                                   ALU.add, waits=[t_sh])
                    t_hq = self.ts(hq[:, hs * INNER:(hs + 1) * INNER], hsl,
                                   -MAGIC, None, ALU.add,
                                   waits=[t_u2, state["t_hq_free"][hs]])
                    state["t_h_free"][ti] = t_hq
                    state["prev_qtok"] = t_cd
                    if tr_out is None:
                        quant_tr([(ti, hs, t_hq)])
                    else:
                        tr_out.append((ti, hs, t_hq))

            def quant_tr(trs):
                for ti, hs, t_hq in trs:
                    t_tp = self.dmat(
                        hqT[:, ti * OC * 128:(ti + 1) * OC * 128]
                        .rearrange("p (oh q) -> p oh q", oh=OC),
                        hq[:, hs * INNER:(hs + 1) * INNER],
                        waits=[t_hq, state["t_hqT_use"][ti]])
                    state["t_hq_free"][hs] = t_tp
                    state["t_hqT_use"][ti] = t_tp

            y1_halves = ([(0, 512), (512, D)] if D > 512 else [(0, D)])

            def load_y1(tsb, half, adv):
                res = {}
                for idx, ti in enumerate(range(half * HB, (half + 1) * HB)):
                    par = (state["ep_i"] + adv + idx) % 2
                    tc = tsb * TSB + ti
                    w = state["z2_users"][par]
                    res[ti] = [self.dma(z2[par][:, a:b],
                                        self.y1_sc[:, tc * D + a:tc * D + b],
                                        waits=w)
                               for a, b in y1_halves]
                return res

            def load_b(ow):
                b = ow % 2
                return self.dma(
                    wdsb[:, b * WOC * D:(b + 1) * WOC * D],
                    self.wT_sc["d"][:, ow * WOC * D:(ow + 1) * WOC * D],
                    waits=[state["t_d_use"][b]])

            def phase_b(tsb, half, pre_ld=None, pre_y=None):
                cdo = (tsb % 2) * TSB
                tis = range(half * HB, (half + 1) * HB)
                t_dn_grp = {ti: [None] * JT for ti in tis}
                t_hqT_tp = {ti: state["t_hqT_use"][ti] for ti in tis}
                for ow in range(OC // WOC):
                    b = ow % 2
                    if pre_ld is not None and ow < len(pre_ld):
                        t_ld = pre_ld[ow]
                    else:
                        t_ld = load_b(ow)
                    for oi in range(WOC):
                        oc = ow * WOC + oi
                        for k, ti in enumerate(tis):
                            for jt in range(JT):
                                bk = 4 + k * JT + jt
                                w0 = [t_ld, t_hqT_tp[ti]]
                                if oc == 0:
                                    w0.append(self.bank_user[bk])
                                lastmm = (k == HB - 1 and jt == JT - 1)
                                wmm = w0 if oc == 0 else (
                                    [t_ld] if (oi == 0 and k == 0 and jt == 0)
                                    else ())
                                t_dn_grp[ti][jt] = self.mm(
                                    self.bank[bk],
                                    hqT[:, ti * OC * 128 + oc * 128:
                                        ti * OC * 128 + oc * 128 + 128],
                                    wdsb[:, b * WOC * D + oi * D + jt * 512:
                                         b * WOC * D + oi * D + jt * 512 + 512],
                                    oc == 0, oc == OC - 1,
                                    waits=wmm,
                                    force_inc=(lastmm and oi == WOC - 1))
                    state["t_d_use"][b] = t_dn_grp[tis[-1] if False else list(tis)[-1]][JT - 1]
                for k, ti in enumerate(tis):
                    state["t_hqT_use"][ti] = t_dn_grp[ti][JT - 1]
                    tc = tsb * TSB + ti
                    par = state["ep_i"] % 2
                    state["ep_i"] += 1
                    po = 12 * par
                    z2p = z2[par]
                    sqp = sqb
                    if pre_y is not None and ti in pre_y:
                        t_yls = pre_y[ti]
                    else:
                        t_yls = [self.dma(
                            z2p[:, a:b],
                            self.y1_sc[:, tc * D + a:tc * D + b],
                            waits=state["z2_users"][par])
                            for a, b in y1_halves]
                    t_zs = []
                    for jt in range(JT):
                        bk = 4 + k * JT + jt
                        t_z = self.stt(z2p[:, jt * 512:(jt + 1) * 512],
                                       self.bank[bk],
                                       cdc[:, cdo + ti:cdo + ti + 1],
                                       z2p[:, jt * 512: jt * 512 + 512],
                                       ALU.mult, ALU.add,
                                       accum_out=st[:, 4 + po + jt:
                                                    5 + po + jt],
                                       waits=[t_dn_grp[ti][jt]] + t_yls +
                                       [state["t_cd_use"][cdo + ti]])
                        self.bank_user[bk] = t_z
                        t_zs.append(t_z)
                    if JT > 1:
                        t_sum = self.tt(st[:, 6 + po:7 + po],
                                        st[:, 4 + po:5 + po],
                                        st[:, 5 + po:6 + po],
                                        ALU.add, waits=t_zs)
                    else:
                        t_sum = self.copy(st[:, 6 + po:7 + po],
                                          st[:, 4 + po:5 + po], waits=t_zs)
                    t_sq = self.act(sqp, z2p, AF.Square,
                                    accum_out=st[:, 7 + po:8 + po],
                                    waits=t_zs)
                    t_mean = self.ts(st[:, 8 + po:9 + po],
                                     st[:, 6 + po:7 + po], 1.0 / D, None,
                                     ALU.mult, waits=[t_sum])
                    t_ms = self.ts(st[:, 9 + po:10 + po],
                                   st[:, 7 + po:8 + po], 1.0 / D, None,
                                   ALU.mult, waits=[t_sq])
                    t_m2 = self.ts(st[:, 10 + po:11 + po],
                                   st[:, 8 + po:9 + po],
                                   st[:, 8 + po:9 + po], None,
                                   ALU.mult, waits=[t_mean])
                    t_var = self.tt(st[:, 11 + po:12 + po],
                                    st[:, 9 + po:10 + po],
                                    st[:, 10 + po:11 + po],
                                    ALU.subtract, waits=[t_ms, t_m2])
                    t_sd = self.act(st[:, 12 + po:13 + po],
                                    st[:, 11 + po:12 + po], AF.Sqrt,
                                    bias=self.epsc, waits=[t_var])
                    t_rs = self.recip(st[:, 13 + po:14 + po],
                                      st[:, 12 + po:13 + po], waits=[t_sd])
                    t_lb = self.stt(st[:, 14 + po:15 + po],
                                    st[:, 8 + po:9 + po], -1.0,
                                    st[:, 13 + po:14 + po],
                                    ALU.mult, ALU.mult, waits=[t_rs])
                    t_y2 = self.ts(z2p, z2p, st[:, 13 + po:14 + po],
                                   st[:, 14 + po:15 + po],
                                   ALU.mult, ALU.add, waits=[t_lb, t_sq])
                    t_st = self.dma(
                        self.y_out[s, tc * 128:(tc + 1) * 128, :], z2p,
                        waits=[t_y2])
                    state["z2_users"][par] = [t_sq, t_y2, t_st]

            pend = None
            pre = None
            for tsb in range(c.NTSB):
                pre_b = [load_b(0), load_b(1)] if pend is not None else None
                pre_y = load_y1(*pend, 0) if pend is not None else None
                hp = phase_a(tsb, pre)
                quant(tsb, hp, range(HB))
                tr23 = []
                quant(tsb, hp, range(HB, TSB), tr_out=tr23)
                if pend is not None:
                    phase_b(*pend, pre_ld=pre_b, pre_y=pre_y)
                pre_b0 = [load_b(0), load_b(1)]
                quant_tr(tr23)
                phase_b(tsb, 0, pre_ld=pre_b0)
                pre = ([load_a(0), load_a(1)]
                       if tsb + 1 < c.NTSB else None)
                pend = (tsb, 1)
            phase_b(*pend, pre_ld=[load_b(0), load_b(1)])

def _make_aux():
    aux = np.zeros((128, 256), np.float32)
    p = np.arange(128)[:, None]
    j = np.arange(128)[None, :]
    aux[:, 0:128] = np.where(p > j, np.float32(-400.0), np.float32(0.0))
    aux[:, 128:256] = np.eye(128, dtype=np.float32)
    return aux


def build_program(cfg=None):
    cfg = cfg or CFG()
    nc = bass.Bass("TRN2", target_bir_lowering=False, debug=False)
    b = Builder(nc, cfg)
    b.build()
    b.p.emit()
    return nc, b


def kernel(L0_attn_x, L0_attn_Wq, L0_attn_Wk, L0_attn_Wv, L0_attn_Wr,
           L0_ffn_w_gate, L0_ffn_w_up, L0_ffn_w_down):
    cfg = CFG()
    nc, _ = build_program(cfg)
    x = np.ascontiguousarray(np.asarray(L0_attn_x, np.float32))
    base = dict(Wq=np.ascontiguousarray(np.asarray(L0_attn_Wq, np.float32)),
                Wk=np.ascontiguousarray(np.asarray(L0_attn_Wk, np.float32)),
                Wv=np.ascontiguousarray(np.asarray(L0_attn_Wv, np.float32)),
                Wr=np.ascontiguousarray(np.asarray(L0_attn_Wr, np.float32)),
                Wg=np.ascontiguousarray(np.asarray(L0_ffn_w_gate, np.float32)),
                Wu=np.ascontiguousarray(np.asarray(L0_ffn_w_up, np.float32)),
                Wd=np.ascontiguousarray(np.asarray(L0_ffn_w_down, np.float32)),
                aux=_make_aux())
    in_maps = []
    for core in range(8):
        m = dict(base)
        m["x"] = np.ascontiguousarray(x[2 * core: 2 * core + 2])
        in_maps.append(m)
    res = run_bass_kernel_spmd(nc, in_maps, core_ids=list(range(8)))
    out = np.concatenate([np.asarray(r["y"], np.float32)
                          for r in res.results], axis=0)
    return out.reshape(16, 2048, 1024)
